# version 03
# baseline (speedup 1.0000x reference)
import contextlib
import numpy as np
import ml_dtypes
import concourse.bass as bass
import concourse.mybir as mybir
import concourse.bass_utils as bass_utils

F32 = mybir.dt.float32
BF16 = mybir.dt.bfloat16
I32 = mybir.dt.int32
AF = mybir.ActivationFunctionType
ALU = mybir.AluOpType
AX = mybir.AxisListType

B, T, C, H, D = 4, 4096, 2048, 16, 128
NC = 8
HPC = 8
FS = HPC * D
TOK = 2048
SHRINKS = [float(s) for s in np.linspace(0.4, 1.0, 16)]
QMAX = 15.0
BLEND_W = float(np.float32(0.5 - 1e-6))
TIE_REL = float(np.float32(1e-6))
ABS_MASK = 0x7FFFFFFF
ENGS = ("sq", "gq", "so", "cc", "pe", "dv", "ac", "po")


class Sem:
    def __init__(self, h, name):
        self.h, self.name, self.n = h, name, 0


class Tracker:
    def __init__(self):
        self.eng = None
        self.ename = None
        self.sems = {}
        self.waited = {}
        self.lastw = {}
        self.readers = {}

    def use(self, eng, ename):
        self.eng, self.ename = eng, ename
        self.waited.setdefault(ename, {})

    def wait(self, semname, val):
        if val <= 0:
            return
        w = self.waited[self.ename]
        if w.get(semname, 0) < val:
            self.eng.wait_ge(self.sems[semname].h, val)
            w[semname] = val

    def _skip(self, sn):
        return sn == "pe" and self.ename == "pe"

    def dep_read(self, keys):
        for k in keys:
            lw = self.lastw.get(k)
            if lw is not None and not self._skip(lw[0]):
                self.wait(*lw)

    def dep_write(self, keys):
        for k in keys:
            lw = self.lastw.get(k)
            if lw is not None and not self._skip(lw[0]):
                self.wait(*lw)
            for sn, v in self.readers.get(k, {}).items():
                if not self._skip(sn):
                    self.wait(sn, v)

    def op(self, fn, reads=(), writes=(), semname=None, step=1, drain=False):
        self.dep_read(reads)
        self.dep_write(writes)
        inst = fn()
        sn = semname or self.ename
        s = self.sems[sn]
        inst.then_inc(s.h, step)
        s.n += step
        for k in writes:
            self.lastw[k] = (sn, s.n)
            self.readers[k] = {}
        for k in reads:
            self.readers.setdefault(k, {})[sn] = s.n
        if drain:
            d = self.eng.drain()
            d.then_inc(s.h, 1)
            s.n += 1
            for k in writes:
                self.lastw[k] = (sn, s.n)
        return s.n


def build_nc():
    nc = bass.Bass("TRN2", target_bir_lowering=False)
    tr = Tracker()

    xT = nc.dram_tensor("xT", [C, T], F32, kind="ExternalInput").ap()
    xrows = nc.dram_tensor("xrows", [TOK, C], F32, kind="ExternalInput").ap()
    wqT = nc.dram_tensor("wqT", [C, FS], F32, kind="ExternalInput").ap()
    wkT = nc.dram_tensor("wkT", [C, FS], F32, kind="ExternalInput").ap()
    wvT = nc.dram_tensor("wvT", [C, FS], F32, kind="ExternalInput").ap()
    wfcq = nc.dram_tensor("wfcq", [1024, 2048], F32, kind="ExternalInput").ap()
    wprojq = nc.dram_tensor("wprojq", [256, 8192], F32, kind="ExternalInput").ap()
    taur = nc.dram_tensor("taur", [128, 1], F32, kind="ExternalInput").ap()
    nur = nc.dram_tensor("nur", [128, 1], F32, kind="ExternalInput").ap()
    msel = nc.dram_tensor("msel", [128, 2], F32, kind="ExternalInput").ap()
    maskin = nc.dram_tensor("maskin", [128, 896], BF16, kind="ExternalInput").ap()
    identin = nc.dram_tensor("identin", [128, 128], F32, kind="ExternalInput").ap()
    shrk = nc.dram_tensor("shrk", [128, 16], F32, kind="ExternalInput").ap()
    shrs = nc.dram_tensor("shrs", [128, 16], F32, kind="ExternalInput").ap()
    out = nc.dram_tensor("out", [TOK, C], F32, kind="ExternalOutput").ap()

    qhatT = nc.dram_tensor("qhatT", [HPC, 128, T], BF16).ap()
    khatT = nc.dram_tensor("khatT", [HPC, 128, T], BF16).ap()
    vhat = nc.dram_tensor("vhat", [HPC, T, D], BF16).ap()
    attnc = nc.dram_tensor("attnc", [T, FS], BF16)
    attng = nc.dram_tensor("attng", [2 * T, FS], BF16)
    cfc = nc.dram_tensor("cfc", [2048, 1024], BF16)
    gfc = nc.dram_tensor("gfc", [8 * 2048, 1024], BF16)
    cproj = nc.dram_tensor("cproj", [8192, 256], BF16)
    gproj = nc.dram_tensor("gproj", [8 * 8192, 256], BF16)
    wfcT = nc.dram_tensor("wfcT", [2048, 8192], BF16).ap()
    wprojT = nc.dram_tensor("wprojT", [8192, 2048], BF16).ap()

    ctx = contextlib.ExitStack()
    with ctx:
        for sname in ENGS:
            tr.sems[sname] = Sem(ctx.enter_context(nc.semaphore(sname)), sname)

        def mksb(st, name, shape, dt):
            return st.enter_context(nc.sbuf_tensor(name, shape, dt))

        mask_s = mksb(ctx, "mask_s", [128, 896], BF16)
        ident_s = mksb(ctx, "ident_s", [128, 128], F32)
        tau_s = mksb(ctx, "tau_s", [128, 1], F32)
        ntau_s = mksb(ctx, "ntau_s", [128, 1], F32)
        nu_s = mksb(ctx, "nu_s", [128, 1], F32)
        half_s = mksb(ctx, "half_s", [128, 1], F32)
        msel_s = mksb(ctx, "msel_s", [128, 2], F32)
        shrk_s = mksb(ctx, "shrk_s", [128, 16], F32)
        shrs_s = mksb(ctx, "shrs_s", [128, 16], F32)

        PS = [ctx.enter_context(nc.psum_tensor(f"ps{i}", [128, 512], F32)) for i in range(8)]

        block = ctx.enter_context(nc.Block())

        def body(ename, f):
            attr = {"sq": "sync", "gq": "gpsimd", "pe": "tensor",
                    "dv": "vector", "ac": "scalar", "po": "gpsimd"}[ename]

            def g(eng):
                tr.use(eng, ename)
                f(eng)
            getattr(block, attr)(g)

        def barrier():
            snap = [(name, tr.sems[name].n) for name in ENGS]
            for ename in ("sq", "gq", "pe", "dv", "ac"):
                def f(e, snap=snap):
                    for sn, v in snap:
                        tr.wait(sn, v)
                body(ename, f)

        def ph0_sync(e):
            for ap_in, buf, key in ((maskin, mask_s, "mask"), (identin, ident_s, "ident"),
                                    (taur, tau_s, "tau"), (nur, nu_s, "nu"),
                                    (msel, msel_s, "msel"),
                                    (shrk, shrk_s, "shrk"), (shrs, shrs_s, "shrs")):
                tr.op(lambda ap_in=ap_in, buf=buf: e.dma_start(out=buf[:], in_=ap_in[:, :]),
                      writes=[key], semname="sq", step=16)
        body("sq", ph0_sync)

        def ph0_dv(e):
            tr.op(lambda: e.memset(half_s[:], 0.5), writes=["half"])
        body("dv", ph0_dv)

        def ph0_ac(e):
            tr.op(lambda: e.activation(ntau_s[:], tau_s[:], AF.Abs), reads=["tau"], writes=["ntau"])
        body("ac", ph0_ac)

        def ph0_dv2(e):
            tr.op(lambda: e.tensor_scalar(ntau_s[:], ntau_s[:], -1.0, None, ALU.mult),
                  reads=["ntau"], writes=["ntau"])
        body("dv", ph0_dv2)

        stQ = contextlib.ExitStack()
        qw_s = mksb(stQ, "qw_s", [128, 2048], F32)
        qu_s = mksb(stQ, "qu_s", [128, 2048], F32)
        qy_s = mksb(stQ, "qy_s", [128, 2048], F32)
        qi_s = mksb(stQ, "qi_s", [128, 2048], I32)
        qrc_s = mksb(stQ, "qrc_s", [128, 2048], F32)
        qe_s = mksb(stQ, "qe_s", [128, 2048], F32)
        qsq_s = mksb(stQ, "qsq_s", [128, 2048], F32)
        qdq_s = mksb(stQ, "qdq_s", [128, 2048], F32)
        qam_s = mksb(stQ, "qam_s", [128, 32], F32)
        qt1_s = mksb(stQ, "qt1_s", [128, 32], F32)
        qrb_s = mksb(stQ, "qrb_s", [128, 32], F32)
        qes_s = mksb(stQ, "qes_s", [128, 32], F32)
        qesc_s = mksb(stQ, "qesc_s", [128, 32], F32)
        qbe_s = mksb(stQ, "qbe_s", [128, 32], F32)
        qbk_s = mksb(stQ, "qbk_s", [128, 32], F32)
        qbs_s = mksb(stQ, "qbs_s", [128, 32], F32)
        qb2_s = mksb(stQ, "qb2_s", [128, 32], F32)
        qbk2_s = mksb(stQ, "qbk2_s", [128, 32], F32)
        qbs2_s = mksb(stQ, "qbs2_s", [128, 32], F32)
        qlo_s = mksb(stQ, "qlo_s", [128, 32], F32)
        qlok_s = mksb(stQ, "qlok_s", [128, 32], F32)
        qlos_s = mksb(stQ, "qlos_s", [128, 32], F32)
        qml_s = mksb(stQ, "qml_s", [128, 32], mybir.dt.uint8)
        qm2_s = mksb(stQ, "qm2_s", [128, 32], mybir.dt.uint8)
        qac_s = mksb(stQ, "qac_s", [128, 32], F32)
        qbc_s = mksb(stQ, "qbc_s", [128, 32], F32)
        qbsc_s = mksb(stQ, "qbsc_s", [128, 32], F32)
        qcv_s = mksb(stQ, "qcv_s", [128, 4], F32)
        qd2_s = mksb(stQ, "qd2_s", [128, 2048], F32)
        qst_s = [mksb(stQ, f"qst{i}_s", [128, 128], BF16) for i in range(2)]

        def qcv_init(e):
            tr.op(lambda: e.memset(qcv_s[:, 0:1], 0.5), writes=["qcv"])
            tr.op(lambda: e.memset(qcv_s[:, 1:2], 1.0), writes=["qcv"])
            tr.op(lambda: e.memset(qcv_s[:, 2:3], 0.0), writes=["qcv"])
        body("dv", qcv_init)

        def quant_unit(src_slice, contrib, cname, crow0, ccol0):
            nb = 32

            def ld(e):
                tr.op(lambda: e.dma_start(out=qw_s[:], in_=src_slice),
                      writes=["qw"], semname="sq", step=16)
            body("sq", ld)

            def dv_setup(e):
                tr.op(lambda: e.tensor_reduce(qam_s[:], qw_s[:].rearrange("p (n k) -> p n k", k=64),
                                              AX.X, ALU.max, apply_absolute_value=True),
                      reads=["qw"], writes=["qam"])
                tr.op(lambda: e.tensor_scalar(qt1_s[:], qam_s[:], 1e-8, None, ALU.max),
                      reads=["qam"], writes=["qt1"])
                tr.op(lambda: e.reciprocal(qrb_s[:], qt1_s[:]),
                      reads=["qt1"], writes=["qrb"])
                tr.op(lambda: e.tensor_scalar(qrb_s[:], qrb_s[:], QMAX, None, ALU.mult),
                      reads=["qrb"], writes=["qrb"])
                tr.op(lambda: e.tensor_tensor(qu_s[:].rearrange("p (n k) -> p n k", k=64),
                                              qw_s[:].rearrange("p (n k) -> p n k", k=64),
                                              qrb_s[:].broadcast_to([128, nb, 64]), ALU.mult),
                      reads=["qw", "qrb"], writes=["qu"])
            body("dv", dv_setup)

            for i, s in enumerate(SHRINKS):
                sf = float(np.float32(s))
                kf = float(np.float32(1.0) / np.float32(s))

                def po_y(e, kf=kf):
                    tr.op(lambda: e.tensor_scalar(qy_s[:], qu_s[:], kf, None, ALU.mult),
                          reads=["qu"], writes=["qy"], semname="po")
                    tr.op(lambda: e.tensor_copy(qi_s[:], qy_s[:]),
                          reads=["qy"], writes=["qi"], semname="po")
                body("po", po_y)

                def dv_rc(e, sf=sf):
                    tr.op(lambda: e.tensor_scalar(qrc_s[:], qi_s[:], QMAX, -QMAX, ALU.min, ALU.max),
                          reads=["qi"], writes=["qrc"])
                    tr.op(lambda: e.scalar_tensor_tensor(qe_s[:], qrc_s[:], sf, qu_s[:],
                                                         ALU.mult, ALU.subtract),
                          reads=["qrc", "qu"], writes=["qe"])
                body("dv", dv_rc)

                def ac_sq(e):
                    tr.op(lambda: e.activation(qsq_s[:], qe_s[:], AF.Square),
                          reads=["qe"], writes=["qsq"])
                body("ac", ac_sq)

                def dv_red(e, i=i):
                    tr.op(lambda: e.tensor_reduce(qes_s[:], qsq_s[:].rearrange("p (n k) -> p n k", k=64),
                                                  AX.X, ALU.add),
                          reads=["qsq"], writes=["qes"])
                    if i == 0:
                        tr.op(lambda: e.tensor_copy(qbe_s[:], qes_s[:]), reads=["qes"], writes=["qbe"])
                        tr.op(lambda: e.tensor_copy(qbk_s[:], shrk_s[:, 0:1].broadcast_to([128, nb])),
                              reads=["shrk"], writes=["qbk"])
                        tr.op(lambda: e.tensor_copy(qbs_s[:], shrs_s[:, 0:1].broadcast_to([128, nb])),
                              reads=["shrs"], writes=["qbs"])
                        tr.op(lambda: e.memset(qb2_s[:], 3.0e38), writes=["qb2"])
                        tr.op(lambda: e.tensor_copy(qbk2_s[:], shrk_s[:, 0:1].broadcast_to([128, nb])),
                              reads=["shrk"], writes=["qbk2"])
                        tr.op(lambda: e.tensor_copy(qbs2_s[:], shrs_s[:, 0:1].broadcast_to([128, nb])),
                              reads=["shrs"], writes=["qbs2"])
                    else:
                        tr.op(lambda: e.tensor_tensor(qml_s[:], qes_s[:], qbe_s[:], ALU.is_lt),
                              reads=["qes", "qbe"], writes=["qml"])
                        tr.op(lambda: e.select(qlo_s[:], qml_s[:], qbe_s[:], qes_s[:]),
                              reads=["qml", "qbe", "qes"], writes=["qlo"])
                        tr.op(lambda i=i: e.select(qlok_s[:], qml_s[:], qbk_s[:],
                                                   shrk_s[:, i:i + 1].broadcast_to([128, nb])),
                              reads=["qml", "qbk", "shrk"], writes=["qlok"])
                        tr.op(lambda i=i: e.select(qlos_s[:], qml_s[:], qbs_s[:],
                                                   shrs_s[:, i:i + 1].broadcast_to([128, nb])),
                              reads=["qml", "qbs", "shrs"], writes=["qlos"])
                        tr.op(lambda: e.tensor_tensor(qm2_s[:], qlo_s[:], qb2_s[:], ALU.is_lt),
                              reads=["qlo", "qb2"], writes=["qm2"])
                        tr.op(lambda: e.tensor_tensor(qb2_s[:], qlo_s[:], qb2_s[:], ALU.min),
                              reads=["qlo", "qb2"], writes=["qb2"])
                        tr.op(lambda: e.select(qbk2_s[:], qm2_s[:], qlok_s[:], qbk2_s[:]),
                              reads=["qm2", "qlok", "qbk2"], writes=["qbk2"])
                        tr.op(lambda: e.select(qbs2_s[:], qm2_s[:], qlos_s[:], qbs2_s[:]),
                              reads=["qm2", "qlos", "qbs2"], writes=["qbs2"])
                        tr.op(lambda: e.tensor_tensor(qbe_s[:], qes_s[:], qbe_s[:], ALU.min),
                              reads=["qes", "qbe"], writes=["qbe"])
                        tr.op(lambda i=i: e.select(qbk_s[:], qml_s[:],
                                                   shrk_s[:, i:i + 1].broadcast_to([128, nb]), qbk_s[:]),
                              reads=["qml", "shrk", "qbk"], writes=["qbk"])
                        tr.op(lambda i=i: e.select(qbs_s[:], qml_s[:],
                                                   shrs_s[:, i:i + 1].broadcast_to([128, nb]), qbs_s[:]),
                              reads=["qml", "shrs", "qbs"], writes=["qbs"])
                body("dv", dv_red)

            def dq_pass(kkey, skey, kap, sap, obuf, okey):
                def dv_d(e):
                    tr.op(lambda: e.tensor_tensor(qy_s[:].rearrange("p (n k) -> p n k", k=64),
                                                  qu_s[:].rearrange("p (n k) -> p n k", k=64),
                                                  kap.broadcast_to([128, nb, 64]), ALU.mult),
                          reads=["qu", kkey], writes=["qy"])
                body("dv", dv_d)

                def po_cast(e):
                    tr.op(lambda: e.tensor_copy(qi_s[:], qy_s[:]),
                          reads=["qy"], writes=["qi"], semname="po")
                    tr.op(lambda: e.tensor_copy(qrc_s[:], qi_s[:]),
                          reads=["qi"], writes=["qrc"], semname="po")
                body("po", po_cast)

                def dv_blend(e):
                    tr.op(lambda: e.tensor_tensor(qe_s[:], qy_s[:], qrc_s[:], ALU.subtract),
                          reads=["qy", "qrc"], writes=["qe"])
                    tr.op(lambda: e.tensor_scalar(qi_s[:], qe_s[:].bitcast(I32), ABS_MASK, None,
                                                  ALU.bitwise_and),
                          reads=["qe"], writes=["qi"])
                    tr.op(lambda: e.tensor_scalar(qml_big, qi_s[:].bitcast(F32), BLEND_W, None,
                                                  ALU.is_ge),
                          reads=["qi"], writes=["qmlb", "qsq"])
                    tr.op(lambda: e.select(qrc_s[:], qml_big, qy_s[:], qrc_s[:]),
                          reads=["qmlb", "qy", "qrc"], writes=["qrc"])
                    tr.op(lambda: e.tensor_scalar(qrc_s[:], qrc_s[:], QMAX, -QMAX, ALU.min, ALU.max),
                          reads=["qrc"], writes=["qrc"])
                    tr.op(lambda: e.tensor_tensor(qbsc_s[:], sap, qt1_s[:], ALU.mult),
                          reads=[skey, "qt1"], writes=["qbsc"])
                    tr.op(lambda: e.tensor_scalar(qbsc_s[:], qbsc_s[:], 1.0 / QMAX, None, ALU.mult),
                          reads=["qbsc"], writes=["qbsc"])
                    tr.op(lambda: e.tensor_tensor(obuf.rearrange("p (n k) -> p n k", k=64),
                                                  qrc_s[:].rearrange("p (n k) -> p n k", k=64),
                                                  qbsc_s[:].broadcast_to([128, nb, 64]), ALU.mult),
                          reads=["qrc", "qbsc"], writes=[okey])
                body("dv", dv_blend)

            qml_big = qe_s[:].bitcast(mybir.dt.uint8)[:, 0:2048]
            qml_big = None

            qml_big = qsq_s[:].bitcast(mybir.dt.uint8)[:, 0:2048]

            dq_pass("qbk", "qbs", qbk_s[:], qbs_s[:], qdq_s[:], "qdq")
            dq_pass("qbk2", "qbs2", qbk2_s[:], qbs2_s[:], qd2_s[:], "qd2")

            def dv_tie(e):
                tr.op(lambda: e.tensor_tensor(qlo_s[:], qb2_s[:], qbe_s[:], ALU.subtract),
                      reads=["qb2", "qbe"], writes=["qlo"])
                tr.op(lambda: e.tensor_scalar(qlos_s[:], qbe_s[:], TIE_REL, None, ALU.mult),
                      reads=["qbe"], writes=["qlos"])
                tr.op(lambda: e.tensor_tensor(qm2_s[:], qlo_s[:], qlos_s[:], ALU.is_le),
                      reads=["qlo", "qlos"], writes=["qm2"])
                tr.op(lambda: e.select(qac_s[:], qm2_s[:], qcv_s[:, 0:1].broadcast_to([128, nb]),
                                       qcv_s[:, 1:2].broadcast_to([128, nb])),
                      reads=["qm2", "qcv"], writes=["qac"])
                tr.op(lambda: e.select(qbc_s[:], qm2_s[:], qcv_s[:, 0:1].broadcast_to([128, nb]),
                                       qcv_s[:, 2:3].broadcast_to([128, nb])),
                      reads=["qm2", "qcv"], writes=["qbc"])
                tr.op(lambda: e.tensor_tensor(qdq_s[:].rearrange("p (n k) -> p n k", k=64),
                                              qdq_s[:].rearrange("p (n k) -> p n k", k=64),
                                              qac_s[:].broadcast_to([128, nb, 64]), ALU.mult),
                      reads=["qdq", "qac"], writes=["qdq"])
                tr.op(lambda: e.tensor_tensor(qd2_s[:].rearrange("p (n k) -> p n k", k=64),
                                              qd2_s[:].rearrange("p (n k) -> p n k", k=64),
                                              qbc_s[:].broadcast_to([128, nb, 64]), ALU.mult),
                      reads=["qd2", "qbc"], writes=["qd2"])
                tr.op(lambda: e.tensor_tensor(qdq_s[:], qdq_s[:], qd2_s[:], ALU.add),
                      reads=["qdq", "qd2"], writes=["qdq"], drain=True)
            body("dv", dv_tie)

            for j in range(16):
                jsel = j % 2

                def pe_tr(e, j=j, jsel=jsel):
                    tr.op(lambda: e.transpose(PS[6 + jsel][:, 0:128], qdq_s[:, 128 * j:128 * (j + 1)],
                                              ident_s[:]),
                          reads=["qdq", "ident"], writes=[f"PS{6 + jsel}"])
                body("pe", pe_tr)

                def dv_cp(e, jsel=jsel):
                    tr.op(lambda: e.tensor_copy(qst_s[jsel][:], PS[6 + jsel][:, 0:128]),
                          reads=[f"PS{6 + jsel}"], writes=[f"qst{jsel}"], drain=True)
                body("dv", dv_cp)

                def st_out(e, j=j, jsel=jsel):
                    tr.op(lambda: e.dma_start(out=contrib[crow0 + 128 * j:crow0 + 128 * (j + 1),
                                                          ccol0:ccol0 + 128],
                                              in_=qst_s[jsel][:]),
                          reads=[f"qst{jsel}"], writes=[cname], semname="sq", step=16)
                body("sq", st_out)

        for t in range(8):
            quant_unit(wfcq[128 * t:128 * (t + 1), :], cfc.ap(), "cfc", 0, 128 * t)

        def ag_fc(e):
            tr.op(lambda: e.collective_compute("AllGather", ALU.bypass,
                                               replica_groups=[list(range(8))],
                                               ins=[cfc.ap().opt()], outs=[gfc.ap().opt()]),
                  reads=["cfc"], writes=["gfc"], semname="cc")
        body("gq", ag_fc)

        for t in range(2):
            for ic in range(4):
                quant_unit(wprojq[128 * t:128 * (t + 1), 2048 * ic:2048 * (ic + 1)],
                           cproj.ap(), "cproj", 2048 * ic, 128 * t)

        def ag_proj(e):
            tr.op(lambda: e.collective_compute("AllGather", ALU.bypass,
                                               replica_groups=[list(range(8))],
                                               ins=[cproj.ap().opt()], outs=[gproj.ap().opt()]),
                  reads=["cproj"], writes=["gproj"], semname="cc")
        body("gq", ag_proj)

        barrier()
        stQ.close()

        stA = contextlib.ExitStack()
        wq_s = mksb(stA, "wq_s", [128, 16, FS], BF16)
        wk_s = mksb(stA, "wk_s", [128, 16, FS], BF16)
        wv_s = mksb(stA, "wv_s", [128, 16, FS], BF16)
        xtb_s = [mksb(stA, f"xtb{i}_s", [128, 16, 512], BF16) for i in range(2)]
        sqsc_s = mksb(stA, "sqsc_s", [128, 512], F32)
        ss_s = mksb(stA, "ss_s", [128, 3, 8], F32)
        ln8_s = mksb(stA, "ln8_s", [128, 8], F32)
        sr8_s = mksb(stA, "sr8_s", [128, 8], F32)
        rs8_s = mksb(stA, "rs8_s", [128, 3, 8], F32)
        tm_s = mksb(stA, "tm_s", [128, 3, 8, 128], F32)
        stq_s = mksb(stA, "stq_s", [128, 8, 512], BF16)
        stk_s = mksb(stA, "stk_s", [128, 8, 512], BF16)

        def lda1w(e):
            for ap_in, buf, key in ((wqT, wq_s, "wq"), (wkT, wk_s, "wk"), (wvT, wv_s, "wv")):
                tr.op(lambda ap_in=ap_in, buf=buf: e.dma_start(
                    out=buf[:], in_=ap_in.rearrange("(i p) f -> p i f", p=128)),
                      writes=[key], semname="gq", step=16)
        body("gq", lda1w)

        for tb in range(8):
            sel = tb % 2

            def ld_x(e, tb=tb, sel=sel):
                tr.op(lambda: e.dma_start(
                    out=xtb_s[sel][:],
                    in_=xT[:, 512 * tb:512 * (tb + 1)].rearrange("(i p) t -> p i t", p=128)),
                      writes=[f"xtb{sel}"], semname="gq", step=16)
            body("gq", ld_x)

            for tt in range(4):
                for P, (wkey, w_s) in enumerate((("wq", wq_s), ("wk", wk_s), ("wv", wv_s))):
                    for fh in range(2):
                        bank = (P * 2 + fh) % 4

                        def mms(e, sel=sel, tt=tt, fh=fh, bank=bank, wkey=wkey, w_s=w_s):
                            for i in range(16):
                                tr.op(lambda i=i: e.matmul(PS[bank][:],
                                                           xtb_s[sel][:, i, 128 * tt:128 * (tt + 1)],
                                                           w_s[:, i, 512 * fh:512 * (fh + 1)],
                                                           start=(i == 0), stop=(i == 15)),
                                      reads=[f"xtb{sel}", wkey], writes=[f"PS{bank}"])
                        body("pe", mms)

                        def acts(e, bank=bank):
                            tr.op(lambda: e.activation(sqsc_s[:], PS[bank][:], AF.Square),
                                  reads=[f"PS{bank}"], writes=["sqsc"])
                        body("ac", acts)

                        def dvs(e, bank=bank, P=P, fh=fh):
                            tr.op(lambda: e.tensor_reduce(ss_s[:, P, 4 * fh:4 * fh + 4],
                                                          sqsc_s[:].rearrange("p (h d) -> p h d", d=128),
                                                          AX.X, ALU.add),
                                  reads=["sqsc"], writes=[f"ss{P}"])
                        body("dv", dvs)

                    def rstd(e, P=P):
                        tr.op(lambda: e.activation(ln8_s[:], ss_s[:, P, :], AF.Ln),
                              reads=[f"ss{P}"], writes=["ln8"])
                        tr.op(lambda: e.activation(sr8_s[:], ln8_s[:], AF.Exp, scale=0.5),
                              reads=["ln8"], writes=["sr8"])
                    body("ac", rstd)

                    def rstd2(e, P=P):
                        tr.op(lambda: e.tensor_scalar(sr8_s[:], sr8_s[:], 1e-8, None, ALU.add),
                              reads=["sr8"], writes=["sr8"])
                        tr.op(lambda: e.reciprocal(rs8_s[:, P, :], sr8_s[:]),
                              reads=["sr8"], writes=[f"rs8{P}"])
                        for fh in range(2):
                            bank = (P * 2 + fh) % 4
                            tr.op(lambda fh=fh, bank=bank: e.tensor_tensor(
                                tm_s[:, P, 4 * fh:4 * fh + 4, :],
                                PS[bank][:].rearrange("p (h d) -> p h d", d=128),
                                rs8_s[:, P, 4 * fh:4 * fh + 4].broadcast_to([128, 4, 128]),
                                ALU.mult),
                                reads=[f"PS{bank}", f"rs8{P}"], writes=[f"tm{P}"], drain=True)
                    body("dv", rstd2)

                def vout(e, tb=tb, tt=tt):
                    for h in range(8):
                        tr.op(lambda h=h: e.dma_start(
                            out=vhat[h, 512 * tb + 128 * tt:512 * tb + 128 * (tt + 1), :],
                            in_=tm_s[:, 2, h, :]),
                              reads=["tm2"], writes=["vhat"], semname="gq", step=16)
                body("gq", vout)

                for P, stg in ((0, stq_s), (1, stk_s)):
                    for h in range(8):
                        psel = 4 + (h % 2)

                        def ptr(e, P=P, h=h, psel=psel):
                            tr.op(lambda: e.transpose(PS[psel][:, 0:128], tm_s[:, P, h, :], ident_s[:]),
                                  reads=[f"tm{P}", "ident"], writes=[f"PS{psel}"])
                        body("pe", ptr)

                        def pcp(e, P=P, h=h, tt=tt, psel=psel, stg=stg):
                            tr.op(lambda: e.tensor_copy(stg[:, h, 128 * tt:128 * (tt + 1)],
                                                        PS[psel][:, 0:128]),
                                  reads=[f"PS{psel}"], writes=[f"st{P}"], drain=True)
                        body("dv", pcp)

            def stout(e, tb=tb):
                for P, (stg, dst) in enumerate(((stq_s, qhatT), (stk_s, khatT))):
                    for h in range(8):
                        tr.op(lambda h=h, stg=stg, dst=dst: e.dma_start(
                            out=dst[h, :, 512 * tb:512 * (tb + 1)], in_=stg[:, h, :]),
                              reads=[f"st{P}"], writes=["qkhatT"], semname="sq", step=16)
            body("sq", stout)

        barrier()
        stA.close()

        stB = contextlib.ExitStack()
        kbuf_s = [mksb(stB, f"kbuf{i}_s", [128, T], BF16) for i in range(2)]
        vbuf_s = [mksb(stB, f"vbuf{i}_s", [128, 32, 129], BF16) for i in range(2)]
        qbuf_s = [mksb(stB, f"qbuf{i}_s", [128, 1024], BF16) for i in range(2)]
        ebuf_s = [mksb(stB, f"ebuf{i}_s", [128, 512], BF16) for i in range(2)]
        rcp_s = mksb(stB, "rcp_s", [128, 1], F32)
        sv_s = mksb(stB, "sv_s", [128, 1], F32)
        attnb_s = mksb(stB, "attnb_s", [128, 8, 1024], F32)

        def vones(e):
            for i in range(2):
                tr.op(lambda i=i: e.memset(vbuf_s[i][:, :, 128:129], 1.0), writes=[f"vones{i}"], drain=True)
        body("dv", vones)

        def rearr(e):
            for r in range(8):
                tr.op(lambda r=r: e.dma_start(out=wfcT[:, 1024 * r:1024 * (r + 1)],
                                              in_=gfc.ap()[2048 * r:2048 * (r + 1), :]),
                      reads=["gfc"], writes=["wfcT"], semname="gq", step=16)
            for r in range(8):
                for q4 in range(4):
                    tr.op(lambda r=r, q4=q4: e.dma_start(
                        out=wprojT[2048 * q4:2048 * (q4 + 1), 256 * r:256 * (r + 1)],
                        in_=gproj.ap()[8192 * r + 2048 * q4:8192 * r + 2048 * (q4 + 1), :]),
                          reads=["gproj"], writes=["wprojT"], semname="gq", step=16)
        body("gq", rearr)

        for sbi in range(4):
            kmax = (sbi + 1) * 1024
            ktall = kmax // 128
            for h in range(8):
                sel = (sbi * 8 + h) % 2

                def lda2(e, h=h, sel=sel, kmax=kmax, ktall=ktall, sbi=sbi):
                    tr.op(lambda: e.dma_start(out=kbuf_s[sel][:, 0:kmax], in_=khatT[h, :, 0:kmax]),
                          reads=["qkhatT"], writes=[f"kbuf{sel}"], semname="sq", step=16)
                    tr.op(lambda: e.dma_start(out=vbuf_s[sel][:, 0:ktall, 0:128],
                                              in_=vhat[h, 0:kmax, :].rearrange("(j p) d -> p j d", p=128)),
                          reads=["vhat"], writes=[f"vbuf{sel}"], semname="sq", step=16)
                    tr.op(lambda: e.dma_start(out=qbuf_s[sel][:],
                                              in_=qhatT[h, :, 1024 * sbi:1024 * (sbi + 1)]),
                          reads=["qkhatT"], writes=[f"qbuf{sel}"], semname="sq", step=16)
                body("sq", lda2)

                for qb in range(2):
                    q0g = 1024 * sbi + 512 * qb
                    kt = (q0g + 512) // 128
                    dstart = q0g // 128
                    for j in range(kt):
                        es = j % 2

                        def smm(e, sel=sel, j=j, es=es, qb=qb):
                            tr.op(lambda: e.matmul(PS[es][:], kbuf_s[sel][:, 128 * j:128 * (j + 1)],
                                                   qbuf_s[sel][:, 512 * qb:512 * (qb + 1)],
                                                   start=True, stop=True),
                                  reads=[f"kbuf{sel}", f"qbuf{sel}"], writes=[f"PS{es}"])
                        body("pe", smm)

                        def eexp(e, es=es):
                            tr.op(lambda: e.activation(ebuf_s[es][:], PS[es][:], AF.Exp,
                                                       scale=tau_s[:, 0:1], bias=ntau_s[:, 0:1]),
                                  reads=[f"PS{es}", "tau", "ntau"], writes=[f"ebuf{es}"], drain=True)
                        body("ac", eexp)

                        if j >= dstart:
                            s_al = j - dstart

                            def emask(e, es=es, s_al=s_al):
                                tr.op(lambda: e.tensor_tensor(
                                    ebuf_s[es][:], ebuf_s[es][:],
                                    mask_s[:, 384 - 128 * s_al:896 - 128 * s_al], ALU.mult),
                                      reads=[f"ebuf{es}", "mask"], writes=[f"ebuf{es}"], drain=True)
                            body("dv", emask)

                        def pvm(e, sel=sel, j=j, es=es, kt=kt):
                            for i in range(4):
                                bank = (2, 3, 6, 7)[i]
                                off = 0
                                tr.op(lambda i=i, bank=bank, off=off: e.matmul(
                                    PS[bank][:, off:off + 129],
                                    ebuf_s[es][:, 128 * i:128 * (i + 1)],
                                    vbuf_s[sel][:, j, :],
                                    start=(j == 0), stop=(j == kt - 1)),
                                      reads=[f"ebuf{es}", f"vbuf{sel}", f"vones{sel}"],
                                      writes=[f"PS{bank}"])
                        body("pe", pvm)

                    for i in range(4):
                        bank = (2, 3, 6, 7)[i]
                        off = 0

                        def norm_dv(e, bank=bank, off=off):
                            tr.op(lambda: e.reciprocal(rcp_s[:], PS[bank][:, off + 128:off + 129]),
                                  reads=[f"PS{bank}"], writes=["rcp"])
                            tr.op(lambda: e.tensor_tensor(sv_s[:], rcp_s[:], nu_s[:], ALU.mult),
                                  reads=["rcp", "nu"], writes=["sv"])
                        body("dv", norm_dv)

                        def norm_ac(e, bank=bank, off=off, qb=qb, i=i, h=h):
                            tr.op(lambda: e.activation(attnb_s[:, 4 * qb + i, 128 * h:128 * (h + 1)],
                                                       PS[bank][:, off:off + 128], AF.Copy,
                                                       scale=sv_s[:, 0:1]),
                                  reads=[f"PS{bank}", "sv"], writes=["attnb"], drain=True)
                        body("ac", norm_ac)

            def attn_out(e, sbi=sbi):
                tr.op(lambda: e.dma_start(out=attnc.ap()[1024 * sbi:1024 * (sbi + 1), :]
                                          .rearrange("(tt p) f -> p tt f", p=128),
                                          in_=attnb_s[:]),
                      reads=["attnb"], writes=["attnc"], semname="gq", step=16)
            body("gq", attn_out)

        def ag_attn(e):
            tr.op(lambda: e.collective_compute("AllGather", ALU.bypass,
                                               replica_groups=[[0, 1], [2, 3], [4, 5], [6, 7]],
                                               ins=[attnc.ap().opt()], outs=[attng.ap().opt()]),
                  reads=["attnc"], writes=["attng"], semname="cc")
        body("gq", ag_attn)

        barrier()
        stB.close()

        stF = contextlib.ExitStack()
        g_s = mksb(stF, "g_s", [128, 64, 512], BF16)
        x1_s = mksb(stF, "x1_s", [128, 4, 2048], F32)
        r2_s = mksb(stF, "r2_s", [128, 4, 2048], F32)
        x1T_s = mksb(stF, "x1T_s", [128, 16, 512], BF16)
        fst_s = [mksb(stF, f"fst{i}_s", [128, 16, 256], BF16) for i in range(2)]
        xr_s = mksb(stF, "xr_s", [128, 2048], F32)
        abA_s = mksb(stF, "abA_s", [128, 2048], BF16)
        abB_s = mksb(stF, "abB_s", [128, 2048], BF16)
        r1_s = mksb(stF, "r1_s", [128, 2048], F32)
        ld_s = mksb(stF, "ld_s", [128, 2048], F32)
        rl_s = mksb(stF, "rl_s", [128, 512], F32)
        pst_s = [mksb(stF, f"pst{i}_s", [128, 1024], BF16) for i in range(2)]
        lsum_s = mksb(stF, "lsum_s", [128, 1], F32)
        lnm_s = mksb(stF, "lnm_s", [128, 1], F32)
        lss_s = mksb(stF, "lss_s", [128, 1], F32)
        lv_s = mksb(stF, "lv_s", [128, 1], F32)
        lr_s = mksb(stF, "lr_s", [128, 1], F32)

        def ln_tail(e, src_key, dst_ap, dst_key):
            tr.op(lambda: e.activation(lr_s[:], lv_s[:], AF.Ln),
                  reads=["lv"], writes=["lr"])
            tr.op(lambda: e.activation(lr_s[:], lr_s[:], AF.Exp, scale=-0.5),
                  reads=["lr"], writes=["lr"])
            tr.op(lambda: e.activation(dst_ap, ld_s[:], AF.Copy, scale=lr_s[:, 0:1]),
                  reads=["ld", "lr"], writes=[dst_key], drain=True)

        for tb4 in range(4):
            tok0 = 512 * tb4

            for tt in range(4):
                r0 = tok0 + 128 * tt

                def ldf(e, r0=r0):
                    tr.op(lambda: e.dma_start(out=xr_s[:], in_=xrows[r0:r0 + 128, :]),
                          writes=["xr"], semname="sq", step=16)
                    for hp in range(2):
                        tr.op(lambda hp=hp: e.dma_start(
                            out=abA_s[:, 1024 * hp:1024 * (hp + 1)],
                            in_=attng.ap()[T * hp + r0:T * hp + r0 + 128, :]),
                              reads=["attng"], writes=["abA"], semname="sq", step=16)
                        tr.op(lambda hp=hp: e.dma_start(
                            out=abB_s[:, 1024 * hp:1024 * (hp + 1)],
                            in_=attng.ap()[T * hp + 2048 + r0:T * hp + 2048 + r0 + 128, :]),
                              reads=["attng"], writes=["abB"], semname="sq", step=16)
                body("sq", ldf)

                def ln1(e, tt=tt):
                    tr.op(lambda: e.scalar_tensor_tensor(r1_s[:], abA_s[:], msel_s[:, 0:1],
                                                         xr_s[:], ALU.mult, ALU.add),
                          reads=["abA", "msel", "xr"], writes=["r1"])
                    tr.op(lambda: e.scalar_tensor_tensor(r1_s[:], abB_s[:], msel_s[:, 1:2],
                                                         r1_s[:], ALU.mult, ALU.add),
                          reads=["abB", "msel", "r1"], writes=["r1"])
                    tr.op(lambda: e.reduce_sum(lsum_s[:], r1_s[:], AX.X),
                          reads=["r1"], writes=["lsum"])
                    tr.op(lambda: e.tensor_scalar(lnm_s[:], lsum_s[:], -1.0 / 2048.0, None, ALU.mult),
                          reads=["lsum"], writes=["lnm"])
                    tr.op(lambda: e.tensor_scalar(ld_s[:], r1_s[:], lnm_s[:, 0:1], None, ALU.add),
                          reads=["r1", "lnm"], writes=["ld"])
                body("dv", ln1)

                def ln1b(e):
                    tr.op(lambda: e.activation(r1_s[:], ld_s[:], AF.Square, accum_out=lss_s[:]),
                          reads=["ld"], writes=["r1", "lss"])
                body("ac", ln1b)

                def ln1c(e):
                    tr.op(lambda: e.tensor_scalar(lv_s[:], lss_s[:], 1.0 / 2048.0, 1e-5,
                                                  ALU.mult, ALU.add),
                          reads=["lss"], writes=["lv"])
                body("dv", ln1c)

                def ln1d(e, tt=tt):
                    ln_tail(e, "ld", x1_s[:, tt, :], "x1")
                body("ac", ln1d)

                for i in range(16):
                    psel = 4 + (i % 2)

                    def xtr(e, i=i, tt=tt, psel=psel):
                        tr.op(lambda: e.transpose(PS[psel][:, 0:128], x1_s[:, tt, 128 * i:128 * (i + 1)],
                                                  ident_s[:]),
                              reads=["x1", "ident"], writes=[f"PS{psel}"])
                    body("pe", xtr)

                    def xcp(e, i=i, tt=tt, psel=psel):
                        tr.op(lambda: e.tensor_copy(x1T_s[:, i, 128 * tt:128 * (tt + 1)],
                                                    PS[psel][:, 0:128]),
                              reads=[f"PS{psel}"], writes=["x1T"], drain=True)
                    body("dv", xcp)

            for ob in range(32):
                fs2 = ob % 2

                def ldw1(e, ob=ob, fs2=fs2):
                    tr.op(lambda: e.dma_start(out=fst_s[fs2][:],
                                              in_=wfcT[:, 256 * ob:256 * (ob + 1)]
                                              .rearrange("(i p) o -> p i o", p=128)),
                          reads=["wfcT"], writes=[f"fst{fs2}"], semname="sq", step=16)
                body("sq", ldw1)

                for oc in range(2):
                    hs = oc % 2

                    def fmm(e, fs2=fs2, oc=oc, hs=hs):
                        for i in range(16):
                            tr.op(lambda i=i: e.matmul(PS[hs][:],
                                                       fst_s[fs2][:, i, 128 * oc:128 * (oc + 1)],
                                                       x1T_s[:, i, :],
                                                       start=(i == 0), stop=(i == 15)),
                                  reads=[f"fst{fs2}", "x1T"], writes=[f"PS{hs}"])
                    body("pe", fmm)

                    def frelu(e, hs=hs, ob=ob, oc=oc):
                        tr.op(lambda: e.activation(rl_s[:], PS[hs][:], AF.Relu),
                              reads=[f"PS{hs}"], writes=["rl"])
                        tr.op(lambda: e.activation(g_s[:, 2 * ob + oc, :], rl_s[:], AF.Square),
                              reads=["rl"], writes=["g"], drain=True)
                    body("ac", frelu)

            for cp in range(2):
                for oc in range(64):
                    ps2 = oc % 2

                    def ldw2(e, cp=cp, oc=oc, ps2=ps2):
                        tr.op(lambda: e.dma_start(out=pst_s[ps2][:],
                                                  in_=wprojT[128 * oc:128 * (oc + 1),
                                                             1024 * cp:1024 * (cp + 1)]),
                              reads=["wprojT"], writes=[f"pst2{ps2}"], semname="sq", step=16)
                    body("sq", ldw2)

                    def ymm(e, oc=oc, ps2=ps2):
                        for tt in range(4):
                            for hf in range(2):
                                bank = tt * 2 + hf
                                tr.op(lambda tt=tt, hf=hf, bank=bank: e.matmul(
                                    PS[bank][:],
                                    g_s[:, oc, 128 * tt:128 * (tt + 1)],
                                    pst_s[ps2][:, 512 * hf:512 * (hf + 1)],
                                    start=(oc == 0), stop=(oc == 63)),
                                      reads=["g", f"pst2{ps2}"], writes=[f"PS{bank}"])
                    body("pe", ymm)

                def yevac(e, cp=cp):
                    for tt in range(4):
                        for hf in range(2):
                            bank = tt * 2 + hf
                            col = 1024 * cp + 512 * hf
                            tr.op(lambda tt=tt, bank=bank, col=col: e.tensor_tensor(
                                r2_s[:, tt, col:col + 512], PS[bank][:],
                                x1_s[:, tt, col:col + 512], ALU.add),
                                  reads=[f"PS{bank}", "x1"], writes=["r2"])
                body("dv", yevac)

            for tt in range(4):
                def ln2(e, tt=tt):
                    tr.op(lambda: e.reduce_sum(lsum_s[:], r2_s[:, tt, :], AX.X),
                          reads=["r2"], writes=["lsum"])
                    tr.op(lambda: e.tensor_scalar(lnm_s[:], lsum_s[:], -1.0 / 2048.0, None, ALU.mult),
                          reads=["lsum"], writes=["lnm"])
                    tr.op(lambda: e.tensor_scalar(ld_s[:], r2_s[:, tt, :], lnm_s[:, 0:1], None,
                                                  ALU.add),
                          reads=["r2", "lnm"], writes=["ld"])
                body("dv", ln2)

                def ln2b(e):
                    tr.op(lambda: e.activation(r1_s[:], ld_s[:], AF.Square, accum_out=lss_s[:]),
                          reads=["ld"], writes=["r1", "lss"])
                body("ac", ln2b)

                def ln2c(e):
                    tr.op(lambda: e.tensor_scalar(lv_s[:], lss_s[:], 1.0 / 2048.0, 1e-5,
                                                  ALU.mult, ALU.add),
                          reads=["lss"], writes=["lv"])
                body("dv", ln2c)

                def ln2d(e):
                    ln_tail(e, "ld", r1_s[:], "r1")
                body("ac", ln2d)

                def stf(e, tt=tt, tok0=tok0):
                    tr.op(lambda: e.dma_start(out=out[tok0 + 128 * tt:tok0 + 128 * (tt + 1), :],
                                              in_=r1_s[:]),
                          reads=["r1"], writes=["out"], semname="so", step=16)
                body("sq", stf)

        barrier()
        stF.close()

        def fini(e):
            tr.wait("so", tr.sems["so"].n)
        body("sq", fini)

    ninst = sum(1 for _ in nc.all_instructions())
    print(f"[kernel] built: {ninst} instructions")
    return nc


_NC_CACHE = None


def kernel(L0_attn_x, L0_attn_wq, L0_attn_wk, L0_attn_wv, L0_attn_tau, L0_attn_nu,
           L0_ffn_w_fc, L0_ffn_w_proj):
    global _NC_CACHE
    x = np.asarray(L0_attn_x, dtype=np.float32)
    wq = np.asarray(L0_attn_wq, dtype=np.float32)
    wk = np.asarray(L0_attn_wk, dtype=np.float32)
    wv = np.asarray(L0_attn_wv, dtype=np.float32)
    tau = np.asarray(L0_attn_tau, dtype=np.float32).reshape(-1)[0]
    nu = np.asarray(L0_attn_nu, dtype=np.float32).reshape(-1)[0]
    wfc = np.asarray(L0_ffn_w_fc, dtype=np.float32)
    wproj = np.asarray(L0_ffn_w_proj, dtype=np.float32)

    xT_b = [np.ascontiguousarray(x[b].T) for b in range(B)]
    wT = [np.ascontiguousarray(wq.T), np.ascontiguousarray(wk.T), np.ascontiguousarray(wv.T)]
    taur = np.ascontiguousarray(np.broadcast_to(np.float32(tau), (128, 1)))
    nur = np.ascontiguousarray(np.broadcast_to(np.float32(nu), (128, 1)))
    mask = (np.arange(896)[None, :] - 384 >= np.arange(128)[:, None]).astype(ml_dtypes.bfloat16)
    ident = np.eye(128, dtype=np.float32)
    shrk = np.ascontiguousarray(np.broadcast_to(
        (np.float32(1.0) / np.asarray(SHRINKS, np.float32))[None, :], (128, 16)))
    shrs = np.ascontiguousarray(np.broadcast_to(np.asarray(SHRINKS, np.float32)[None, :], (128, 16)))

    in_maps = []
    for c in range(NC):
        b, hh = c // 2, c % 2
        ms = np.zeros((128, 2), np.float32)
        ms[:, 0] = 1.0 - hh
        ms[:, 1] = float(hh)
        in_maps.append(dict(
            xT=xT_b[b],
            xrows=np.ascontiguousarray(x[b, TOK * hh:TOK * (hh + 1)]),
            wqT=np.ascontiguousarray(wT[0][:, FS * hh:FS * (hh + 1)]),
            wkT=np.ascontiguousarray(wT[1][:, FS * hh:FS * (hh + 1)]),
            wvT=np.ascontiguousarray(wT[2][:, FS * hh:FS * (hh + 1)]),
            wfcq=np.ascontiguousarray(wfc[1024 * c:1024 * (c + 1)]),
            wprojq=np.ascontiguousarray(wproj[256 * c:256 * (c + 1)]),
            taur=taur, nur=nur, msel=ms, maskin=mask, identin=ident, shrk=shrk, shrs=shrs,
        ))

    if _NC_CACHE is None:
        _NC_CACHE = build_nc()
    res = bass_utils.run_bass_kernel_spmd(_NC_CACHE, in_maps, core_ids=list(range(NC)))

    y = np.empty((B, T, C), dtype=np.float32)
    for c in range(NC):
        b, hh = c // 2, c % 2
        y[b, TOK * hh:TOK * (hh + 1)] = res.results[c]["out"]
    return y


# version 09 (final)
# speedup vs baseline: 1.3337x; 1.3337x over previous
import contextlib
import numpy as np
import ml_dtypes
import concourse.bass as bass
import concourse.mybir as mybir
import concourse.bass_utils as bass_utils

F32 = mybir.dt.float32
BF16 = mybir.dt.bfloat16
I32 = mybir.dt.int32
AF = mybir.ActivationFunctionType
ALU = mybir.AluOpType
AX = mybir.AxisListType

B, T, C, H, D = 4, 4096, 2048, 16, 128
NC = 8
HPC = 8
FS = HPC * D
TOK = 2048
SHRINKS = [float(s) for s in np.linspace(0.4, 1.0, 16)]
QMAX = 15.0
BLEND_W = float(np.float32(0.5 - 1e-6))
TIE_REL = float(np.float32(1e-6))
ABS_MASK = 0x7FFFFFFF
ENGS = ("sq", "gq", "so", "cc", "pe", "dv", "ac", "po")


class Sem:
    def __init__(self, h, name):
        self.h, self.name, self.n = h, name, 0


class Tracker:
    COMPUTE = ("dv", "ac", "po")

    def __init__(self):
        self.eng = None
        self.ename = None
        self.sems = {}
        self.waited = {}
        self.lastw = {}
        self.readers = {}
        self.flushers = {}
        self.opid = 0
        self.dry = True
        self.need_flush = set()

    def next_pass(self):
        self.dry = False
        self.opid = 0
        self.waited = {}
        self.lastw = {}
        self.readers = {}

    def use(self, eng, ename):
        self.eng, self.ename = eng, ename
        self.waited.setdefault(ename, {})

    def wait(self, semname, val):
        if val <= 0:
            return
        w = self.waited[self.ename]
        if w.get(semname, 0) < val:
            self.eng.wait_ge(self.sems[semname].h, val)
            w[semname] = val

    def _skip(self, sn):
        return sn == "pe" and self.ename == "pe"

    def _mark(self, lw):
        if self.dry and lw[0] in self.COMPUTE and lw[0] != self.ename:
            self.need_flush.add(lw[2])

    def dep_read(self, keys):
        for k in keys:
            lw = self.lastw.get(k)
            if lw is not None and not self._skip(lw[0]):
                self._mark(lw)
                self.wait(lw[0], lw[1])

    def dep_write(self, keys):
        for k in keys:
            lw = self.lastw.get(k)
            if lw is not None and not self._skip(lw[0]):
                self._mark(lw)
                self.wait(lw[0], lw[1])
            for sn, v in self.readers.get(k, {}).items():
                if not self._skip(sn):
                    self.wait(sn, v)

    def op(self, fn, reads=(), writes=(), semname=None, step=1, drain=False):
        myid = self.opid
        self.opid += 1
        self.dep_read(reads)
        self.dep_write(writes)
        inst = fn()
        sn = semname or self.ename
        s = self.sems[sn]
        inst.then_inc(s.h, step)
        s.n += step
        for k in writes:
            self.lastw[k] = (sn, s.n, myid)
            self.readers[k] = {}
        for k in reads:
            self.readers.setdefault(k, {})[sn] = s.n
        if (not self.dry) and (myid in self.need_flush) and sn in self.COMPUTE:
            d = self.flushers[self.ename]()
            d.then_inc(s.h, 1)
            s.n += 1
            for k in writes:
                self.lastw[k] = (sn, s.n, myid)
        return s.n


def build_nc(_tr_state=None):
    nc = bass.Bass("TRN2", target_bir_lowering=False)
    tr = _tr_state or Tracker()

    xT = nc.dram_tensor("xT", [C, T], F32, kind="ExternalInput").ap()
    xrows = nc.dram_tensor("xrows", [TOK, C], F32, kind="ExternalInput").ap()
    wqT = nc.dram_tensor("wqT", [C, FS], F32, kind="ExternalInput").ap()
    wkT = nc.dram_tensor("wkT", [C, FS], F32, kind="ExternalInput").ap()
    wvT = nc.dram_tensor("wvT", [C, FS], F32, kind="ExternalInput").ap()
    wfcq = nc.dram_tensor("wfcq", [1024, 2048], F32, kind="ExternalInput").ap()
    wprojq = nc.dram_tensor("wprojq", [256, 8192], F32, kind="ExternalInput").ap()
    taur = nc.dram_tensor("taur", [128, 1], F32, kind="ExternalInput").ap()
    nur = nc.dram_tensor("nur", [128, 1], F32, kind="ExternalInput").ap()
    msel = nc.dram_tensor("msel", [128, 2], F32, kind="ExternalInput").ap()
    maskin = nc.dram_tensor("maskin", [128, 896], BF16, kind="ExternalInput").ap()
    identin = nc.dram_tensor("identin", [128, 128], F32, kind="ExternalInput").ap()
    shrk = nc.dram_tensor("shrk", [128, 16], F32, kind="ExternalInput").ap()
    shrs = nc.dram_tensor("shrs", [128, 16], F32, kind="ExternalInput").ap()
    out = nc.dram_tensor("out", [TOK, C], F32, kind="ExternalOutput").ap()

    qhatT = nc.dram_tensor("qhatT", [HPC, 128, T], BF16).ap()
    khatT = nc.dram_tensor("khatT", [HPC, 128, T], BF16).ap()
    vhat = nc.dram_tensor("vhat", [HPC, T, D], BF16).ap()
    attnc = nc.dram_tensor("attnc", [T, FS], BF16)
    attng = nc.dram_tensor("attng", [2 * T, FS], BF16)
    cfc = nc.dram_tensor("cfc", [2048, 1024], BF16)
    gfc = nc.dram_tensor("gfc", [8 * 2048, 1024], BF16)
    cproj = nc.dram_tensor("cproj", [8192, 256], BF16)
    gproj = nc.dram_tensor("gproj", [8 * 8192, 256], BF16)
    wfcT = nc.dram_tensor("wfcT", [2048, 8192], BF16).ap()
    wprojT = nc.dram_tensor("wprojT", [8192, 2048], BF16).ap()

    ctx = contextlib.ExitStack()
    with ctx:
        for sname in ENGS:
            tr.sems[sname] = Sem(ctx.enter_context(nc.semaphore(sname)), sname)

        def mksb(st, name, shape, dt):
            return st.enter_context(nc.sbuf_tensor(name, shape, dt))

        mask_s = mksb(ctx, "mask_s", [128, 896], BF16)
        ident_s = mksb(ctx, "ident_s", [128, 128], F32)
        tau_s = mksb(ctx, "tau_s", [128, 1], F32)
        ntau_s = mksb(ctx, "ntau_s", [128, 1], F32)
        nu_s = mksb(ctx, "nu_s", [128, 1], F32)
        half_s = mksb(ctx, "half_s", [128, 1], F32)
        msel_s = mksb(ctx, "msel_s", [128, 2], F32)
        shrk_s = mksb(ctx, "shrk_s", [128, 16], F32)
        shrs_s = mksb(ctx, "shrs_s", [128, 16], F32)
        fdv_s = mksb(ctx, "fdv_s", [128, 1], F32)
        fac_s = mksb(ctx, "fac_s", [128, 1], F32)
        fpo_s = mksb(ctx, "fpo_s", [128, 1], F32)
        tr.flushers = {
            "dv": lambda: tr.eng.memset(fdv_s[:], 0.0),
            "ac": lambda: tr.eng.copy(fac_s[:], half_s[:]),
            "po": lambda: tr.eng.memset(fpo_s[:], 0.0),
        }

        PS = [ctx.enter_context(nc.psum_tensor(f"ps{i}", [128, 512], F32)) for i in range(8)]

        block = ctx.enter_context(nc.Block())

        def body(ename, f):
            attr = {"sq": "sync", "gq": "gpsimd", "pe": "tensor",
                    "dv": "vector", "ac": "scalar", "po": "gpsimd"}[ename]

            def g(eng):
                tr.use(eng, ename)
                f(eng)
            getattr(block, attr)(g)

        def barrier():
            snap = [(name, tr.sems[name].n) for name in ENGS if name not in ("cc", "so")]
            for ename in ("sq", "gq", "pe", "dv", "ac"):
                def f(e, snap=snap):
                    for sn, v in snap:
                        tr.wait(sn, v)
                body(ename, f)

        def ph0_sync(e):
            for ap_in, buf, key in ((maskin, mask_s, "mask"), (identin, ident_s, "ident"),
                                    (taur, tau_s, "tau"), (nur, nu_s, "nu"),
                                    (msel, msel_s, "msel"),
                                    (shrk, shrk_s, "shrk"), (shrs, shrs_s, "shrs")):
                tr.op(lambda ap_in=ap_in, buf=buf: e.dma_start(out=buf[:], in_=ap_in[:, :]),
                      writes=[key], semname="sq", step=16)
        body("sq", ph0_sync)

        def ph0_dv(e):
            tr.op(lambda: e.memset(half_s[:], 0.5), writes=["half"])
        body("dv", ph0_dv)

        def ph0_ac(e):
            tr.op(lambda: e.activation(ntau_s[:], tau_s[:], AF.Abs), reads=["tau"], writes=["ntau"])
        body("ac", ph0_ac)

        def ph0_dv2(e):
            tr.op(lambda: e.tensor_scalar(ntau_s[:], ntau_s[:], -1.0, None, ALU.mult),
                  reads=["ntau"], writes=["ntau"])
        body("dv", ph0_dv2)

        stQ = None

        stA = contextlib.ExitStack()
        QI = 1024
        qw_s = mksb(stA, "qw_s", [128, QI], F32)
        qu_s = mksb(stA, "qu_s", [128, QI], F32)
        qy_s = mksb(stA, "qy_s", [128, QI], F32)
        qi_s = mksb(stA, "qi_s", [128, QI], I32)
        qrc_s = mksb(stA, "qrc_s", [128, QI], F32)
        qe_s = mksb(stA, "qe_s", [128, QI], F32)
        qsq_s = mksb(stA, "qsq_s", [128, QI], F32)
        qdq_s = mksb(stA, "qdq_s", [128, QI], F32)
        qd2_s = mksb(stA, "qd2_s", [128, QI], F32)
        qam_s = mksb(stA, "qam_s", [128, 16], F32)
        qt1_s = mksb(stA, "qt1_s", [128, 16], F32)
        qrb_s = mksb(stA, "qrb_s", [128, 16], F32)
        qes_s = mksb(stA, "qes_s", [128, 16], F32)
        qesc_s = mksb(stA, "qesc_s", [128, 16], F32)
        qbe_s = mksb(stA, "qbe_s", [128, 16], F32)
        qbk_s = mksb(stA, "qbk_s", [128, 16], F32)
        qbs_s = mksb(stA, "qbs_s", [128, 16], F32)
        qb2_s = mksb(stA, "qb2_s", [128, 16], F32)
        qbk2_s = mksb(stA, "qbk2_s", [128, 16], F32)
        qbs2_s = mksb(stA, "qbs2_s", [128, 16], F32)
        qlo_s = mksb(stA, "qlo_s", [128, 16], F32)
        qlok_s = mksb(stA, "qlok_s", [128, 16], F32)
        qlos_s = mksb(stA, "qlos_s", [128, 16], F32)
        qml_s = mksb(stA, "qml_s", [128, 16], mybir.dt.uint8)
        qm2_s = mksb(stA, "qm2_s", [128, 16], mybir.dt.uint8)
        qac_s = mksb(stA, "qac_s", [128, 16], F32)
        qbc_s = mksb(stA, "qbc_s", [128, 16], F32)
        qbsc_s = mksb(stA, "qbsc_s", [128, 16], F32)
        qcv_s = mksb(stA, "qcv_s", [128, 4], F32)
        qst_s = [mksb(stA, f"qst{i}_s", [128, 128], BF16) for i in range(2)]

        def qcv_init(e):
            tr.op(lambda: e.memset(qcv_s[:, 0:1], 0.5), writes=["qcv"])
            tr.op(lambda: e.memset(qcv_s[:, 1:2], 1.0), writes=["qcv"])
            tr.op(lambda: e.memset(qcv_s[:, 2:3], 0.0), writes=["qcv"])
        body("dv", qcv_init)

        def quant_unit(src_slice, contrib, cname, crow0, ccol0):
            nb = 16

            def ld(e):
                tr.op(lambda: e.dma_start(out=qw_s[:], in_=src_slice),
                      writes=["qw"], semname="sq", step=16)
            body("sq", ld)

            def dv_setup(e):
                tr.op(lambda: e.tensor_reduce(qam_s[:], qw_s[:].rearrange("p (n k) -> p n k", k=64),
                                              AX.X, ALU.max, apply_absolute_value=True),
                      reads=["qw"], writes=["qam"])
                tr.op(lambda: e.tensor_scalar(qt1_s[:], qam_s[:], 1e-8, None, ALU.max),
                      reads=["qam"], writes=["qt1"])
                tr.op(lambda: e.reciprocal(qrb_s[:], qt1_s[:]),
                      reads=["qt1"], writes=["qrb"])
                tr.op(lambda: e.tensor_scalar(qrb_s[:], qrb_s[:], QMAX, None, ALU.mult),
                      reads=["qrb"], writes=["qrb"])
                tr.op(lambda: e.tensor_tensor(qu_s[:].rearrange("p (n k) -> p n k", k=64),
                                              qw_s[:].rearrange("p (n k) -> p n k", k=64),
                                              qrb_s[:].broadcast_to([128, nb, 64]), ALU.mult),
                      reads=["qw", "qrb"], writes=["qu"])
            body("dv", dv_setup)

            for i, s in enumerate(SHRINKS):
                sf = float(np.float32(s))
                kf = float(np.float32(1.0) / np.float32(s))

                def dv_err(e, i=i, sf=sf, kf=kf):
                    tr.op(lambda: e.tensor_scalar(qy_s[:], qu_s[:], kf, None, ALU.mult),
                          reads=["qu"], writes=["qy"])
                    tr.op(lambda: e.tensor_copy(qi_s[:], qy_s[:]),
                          reads=["qy"], writes=["qi"])
                    tr.op(lambda: e.tensor_scalar(qrc_s[:], qi_s[:], QMAX, -QMAX, ALU.min, ALU.max),
                          reads=["qi"], writes=["qrc"])
                    tr.op(lambda: e.scalar_tensor_tensor(qe_s[:], qrc_s[:], sf, qu_s[:],
                                                         ALU.mult, ALU.subtract),
                          reads=["qrc", "qu"], writes=["qe"])
                body("dv", dv_err)

                def ac_sq(e):
                    tr.op(lambda: e.activation(qsq_s[:], qe_s[:], AF.Square),
                          reads=["qe"], writes=["qsq"])
                body("ac", ac_sq)

                def dv_red(e, i=i):
                    tr.op(lambda: e.tensor_reduce(qes_s[:], qsq_s[:].rearrange("p (n k) -> p n k", k=64),
                                                  AX.X, ALU.add),
                          reads=["qsq"], writes=["qes"])
                    if i == 0:
                        tr.op(lambda: e.tensor_copy(qbe_s[:], qes_s[:]), reads=["qes"], writes=["qbe"])
                        tr.op(lambda: e.tensor_copy(qbk_s[:], shrk_s[:, 0:1].broadcast_to([128, nb])),
                              reads=["shrk"], writes=["qbk"])
                        tr.op(lambda: e.tensor_copy(qbs_s[:], shrs_s[:, 0:1].broadcast_to([128, nb])),
                              reads=["shrs"], writes=["qbs"])
                        tr.op(lambda: e.memset(qb2_s[:], 3.0e38), writes=["qb2"])
                        tr.op(lambda: e.tensor_copy(qbk2_s[:], shrk_s[:, 0:1].broadcast_to([128, nb])),
                              reads=["shrk"], writes=["qbk2"])
                        tr.op(lambda: e.tensor_copy(qbs2_s[:], shrs_s[:, 0:1].broadcast_to([128, nb])),
                              reads=["shrs"], writes=["qbs2"])
                    else:
                        tr.op(lambda: e.tensor_tensor(qml_s[:], qes_s[:], qbe_s[:], ALU.is_lt),
                              reads=["qes", "qbe"], writes=["qml"])
                        tr.op(lambda: e.select(qlo_s[:], qml_s[:], qbe_s[:], qes_s[:]),
                              reads=["qml", "qbe", "qes"], writes=["qlo"])
                        tr.op(lambda i=i: e.select(qlok_s[:], qml_s[:], qbk_s[:],
                                                   shrk_s[:, i:i + 1].broadcast_to([128, nb])),
                              reads=["qml", "qbk", "shrk"], writes=["qlok"])
                        tr.op(lambda i=i: e.select(qlos_s[:], qml_s[:], qbs_s[:],
                                                   shrs_s[:, i:i + 1].broadcast_to([128, nb])),
                              reads=["qml", "qbs", "shrs"], writes=["qlos"])
                        tr.op(lambda: e.tensor_tensor(qm2_s[:], qlo_s[:], qb2_s[:], ALU.is_lt),
                              reads=["qlo", "qb2"], writes=["qm2"])
                        tr.op(lambda: e.tensor_tensor(qb2_s[:], qlo_s[:], qb2_s[:], ALU.min),
                              reads=["qlo", "qb2"], writes=["qb2"])
                        tr.op(lambda: e.select(qbk2_s[:], qm2_s[:], qlok_s[:], qbk2_s[:]),
                              reads=["qm2", "qlok", "qbk2"], writes=["qbk2"])
                        tr.op(lambda: e.select(qbs2_s[:], qm2_s[:], qlos_s[:], qbs2_s[:]),
                              reads=["qm2", "qlos", "qbs2"], writes=["qbs2"])
                        tr.op(lambda: e.tensor_tensor(qbe_s[:], qes_s[:], qbe_s[:], ALU.min),
                              reads=["qes", "qbe"], writes=["qbe"])
                        tr.op(lambda i=i: e.select(qbk_s[:], qml_s[:],
                                                   shrk_s[:, i:i + 1].broadcast_to([128, nb]), qbk_s[:]),
                              reads=["qml", "shrk", "qbk"], writes=["qbk"])
                        tr.op(lambda i=i: e.select(qbs_s[:], qml_s[:],
                                                   shrs_s[:, i:i + 1].broadcast_to([128, nb]), qbs_s[:]),
                              reads=["qml", "shrs", "qbs"], writes=["qbs"])
                body("dv", dv_red)

            qml_big = qsq_s[:].bitcast(mybir.dt.uint8)[:, 0:QI]

            def dq_pass(kkey, skey, kap, sap, obuf, okey):
                def dv_d(e):
                    tr.op(lambda: e.tensor_tensor(qy_s[:].rearrange("p (n k) -> p n k", k=64),
                                                  qu_s[:].rearrange("p (n k) -> p n k", k=64),
                                                  kap.broadcast_to([128, nb, 64]), ALU.mult),
                          reads=["qu", kkey], writes=["qy"])
                    tr.op(lambda: e.tensor_copy(qi_s[:], qy_s[:]),
                          reads=["qy"], writes=["qi"])
                    tr.op(lambda: e.tensor_copy(qrc_s[:], qi_s[:]),
                          reads=["qi"], writes=["qrc"])
                    tr.op(lambda: e.tensor_tensor(qe_s[:], qy_s[:], qrc_s[:], ALU.subtract),
                          reads=["qy", "qrc"], writes=["qe"])
                    tr.op(lambda: e.tensor_scalar(qi_s[:], qe_s[:].bitcast(I32), ABS_MASK, None,
                                                  ALU.bitwise_and),
                          reads=["qe"], writes=["qi"])
                    tr.op(lambda: e.tensor_scalar(qml_big, qi_s[:].bitcast(F32), BLEND_W, None,
                                                  ALU.is_ge),
                          reads=["qi"], writes=["qmlb", "qsq"])
                    tr.op(lambda: e.select(qrc_s[:], qml_big, qy_s[:], qrc_s[:]),
                          reads=["qmlb", "qy", "qrc"], writes=["qrc"])
                    tr.op(lambda: e.tensor_scalar(qrc_s[:], qrc_s[:], QMAX, -QMAX, ALU.min, ALU.max),
                          reads=["qrc"], writes=["qrc"])
                    tr.op(lambda: e.tensor_tensor(qbsc_s[:], sap, qt1_s[:], ALU.mult),
                          reads=[skey, "qt1"], writes=["qbsc"])
                    tr.op(lambda: e.tensor_scalar(qbsc_s[:], qbsc_s[:], 1.0 / QMAX, None, ALU.mult),
                          reads=["qbsc"], writes=["qbsc"])
                    tr.op(lambda: e.tensor_tensor(obuf.rearrange("p (n k) -> p n k", k=64),
                                                  qrc_s[:].rearrange("p (n k) -> p n k", k=64),
                                                  qbsc_s[:].broadcast_to([128, nb, 64]), ALU.mult),
                          reads=["qrc", "qbsc"], writes=[okey])
                body("dv", dv_d)

            dq_pass("qbk", "qbs", qbk_s[:], qbs_s[:], qdq_s[:], "qdq")
            dq_pass("qbk2", "qbs2", qbk2_s[:], qbs2_s[:], qd2_s[:], "qd2")

            def dv_tie(e):
                tr.op(lambda: e.tensor_tensor(qlo_s[:], qb2_s[:], qbe_s[:], ALU.subtract),
                      reads=["qb2", "qbe"], writes=["qlo"])
                tr.op(lambda: e.tensor_scalar(qlos_s[:], qbe_s[:], TIE_REL, None, ALU.mult),
                      reads=["qbe"], writes=["qlos"])
                tr.op(lambda: e.tensor_tensor(qm2_s[:], qlo_s[:], qlos_s[:], ALU.is_le),
                      reads=["qlo", "qlos"], writes=["qm2"])
                tr.op(lambda: e.select(qac_s[:], qm2_s[:], qcv_s[:, 0:1].broadcast_to([128, nb]),
                                       qcv_s[:, 1:2].broadcast_to([128, nb])),
                      reads=["qm2", "qcv"], writes=["qac"])
                tr.op(lambda: e.select(qbc_s[:], qm2_s[:], qcv_s[:, 0:1].broadcast_to([128, nb]),
                                       qcv_s[:, 2:3].broadcast_to([128, nb])),
                      reads=["qm2", "qcv"], writes=["qbc"])
                tr.op(lambda: e.tensor_tensor(qdq_s[:].rearrange("p (n k) -> p n k", k=64),
                                              qdq_s[:].rearrange("p (n k) -> p n k", k=64),
                                              qac_s[:].broadcast_to([128, nb, 64]), ALU.mult),
                      reads=["qdq", "qac"], writes=["qdq"])
                tr.op(lambda: e.tensor_tensor(qd2_s[:].rearrange("p (n k) -> p n k", k=64),
                                              qd2_s[:].rearrange("p (n k) -> p n k", k=64),
                                              qbc_s[:].broadcast_to([128, nb, 64]), ALU.mult),
                      reads=["qd2", "qbc"], writes=["qd2"])
                tr.op(lambda: e.tensor_tensor(qdq_s[:], qdq_s[:], qd2_s[:], ALU.add),
                      reads=["qdq", "qd2"], writes=["qdq"], drain=True)
            body("dv", dv_tie)

            for j in range(8):
                jsel = j % 2

                def pe_tr(e, j=j, jsel=jsel):
                    tr.op(lambda: e.transpose(PS[6 + jsel][:, 0:128], qdq_s[:, 128 * j:128 * (j + 1)],
                                              ident_s[:]),
                          reads=["qdq", "ident"], writes=[f"PS{6 + jsel}"])
                body("pe", pe_tr)

                def dv_cp(e, jsel=jsel):
                    tr.op(lambda: e.tensor_copy(qst_s[jsel][:], PS[6 + jsel][:, 0:128]),
                          reads=[f"PS{6 + jsel}"], writes=[f"qst{jsel}"], drain=True)
                body("dv", dv_cp)

                def st_out(e, j=j, jsel=jsel):
                    tr.op(lambda: e.dma_start(out=contrib[crow0 + 128 * j:crow0 + 128 * (j + 1),
                                                          ccol0:ccol0 + 128],
                                              in_=qst_s[jsel][:]),
                          reads=[f"qst{jsel}"], writes=[cname], semname="sq", step=16)
                body("sq", st_out)

        quant_units = []
        for t in range(8):
            for ic in range(2):
                quant_units.append((wfcq[128 * t:128 * (t + 1), 1024 * ic:1024 * (ic + 1)],
                                    cfc.ap(), "cfc", 1024 * ic, 128 * t))
        for t in range(2):
            for ic in range(8):
                quant_units.append((wprojq[128 * t:128 * (t + 1), 1024 * ic:1024 * (ic + 1)],
                                    cproj.ap(), "cproj", 1024 * ic, 128 * t))

        def ag_fc(e):
            tr.op(lambda: e.collective_compute("AllGather", ALU.bypass,
                                               replica_groups=[list(range(8))],
                                               ins=[cfc.ap().opt()], outs=[gfc.ap().opt()]),
                  reads=["cfc"], writes=["gfc"], semname="cc")

        def ag_proj(e):
            tr.op(lambda: e.collective_compute("AllGather", ALU.bypass,
                                               replica_groups=[list(range(8))],
                                               ins=[cproj.ap().opt()], outs=[gproj.ap().opt()]),
                  reads=["cproj"], writes=["gproj"], semname="cc")

        for qu_args in quant_units[:16]:
            quant_unit(*qu_args)
        body("gq", ag_fc)
        for qu_args in quant_units[16:]:
            quant_unit(*qu_args)
        body("gq", ag_proj)

        wq_s = mksb(stA, "wq_s", [128, 16, FS], BF16)
        wk_s = mksb(stA, "wk_s", [128, 16, FS], BF16)
        wv_s = mksb(stA, "wv_s", [128, 16, FS], BF16)
        xtb_s = [mksb(stA, f"xtb{i}_s", [128, 16, 512], BF16) for i in range(2)]
        sqsc_s = mksb(stA, "sqsc_s", [128, 512], F32)
        ss_s = mksb(stA, "ss_s", [128, 3, 8], F32)
        ln8_s = mksb(stA, "ln8_s", [128, 8], F32)
        sr8_s = mksb(stA, "sr8_s", [128, 8], F32)
        rs8_s = mksb(stA, "rs8_s", [128, 3, 8], F32)
        tm_s = mksb(stA, "tm_s", [128, 3, 8, 128], F32)
        stq_s = mksb(stA, "stq_s", [128, 8, 512], BF16)
        stk_s = mksb(stA, "stk_s", [128, 8, 512], BF16)

        def lda1w(e):
            for ap_in, buf, key in ((wqT, wq_s, "wq"), (wkT, wk_s, "wk"), (wvT, wv_s, "wv")):
                tr.op(lambda ap_in=ap_in, buf=buf: e.dma_start(
                    out=buf[:], in_=ap_in.rearrange("(i p) f -> p i f", p=128)),
                      writes=[key], semname="gq", step=16)
        body("gq", lda1w)

        for tb in range(8):
            sel = tb % 2

            def ld_x(e, tb=tb, sel=sel):
                tr.op(lambda: e.dma_start(
                    out=xtb_s[sel][:],
                    in_=xT[:, 512 * tb:512 * (tb + 1)].rearrange("(i p) t -> p i t", p=128)),
                      writes=[f"xtb{sel}"], semname="gq", step=16)
            body("gq", ld_x)

            for tt in range(4):
                for P, (wkey, w_s) in enumerate((("wq", wq_s), ("wk", wk_s), ("wv", wv_s))):
                    for fh in range(2):
                        bank = (P * 2 + fh) % 4

                        def mms(e, sel=sel, tt=tt, fh=fh, bank=bank, wkey=wkey, w_s=w_s):
                            for i in range(16):
                                tr.op(lambda i=i: e.matmul(PS[bank][:],
                                                           xtb_s[sel][:, i, 128 * tt:128 * (tt + 1)],
                                                           w_s[:, i, 512 * fh:512 * (fh + 1)],
                                                           start=(i == 0), stop=(i == 15)),
                                      reads=[f"xtb{sel}", wkey], writes=[f"PS{bank}"])
                        body("pe", mms)

                        def acts(e, bank=bank):
                            tr.op(lambda: e.activation(sqsc_s[:], PS[bank][:], AF.Square),
                                  reads=[f"PS{bank}"], writes=["sqsc"])
                        body("ac", acts)

                        def dvs(e, bank=bank, P=P, fh=fh):
                            tr.op(lambda: e.tensor_reduce(ss_s[:, P, 4 * fh:4 * fh + 4],
                                                          sqsc_s[:].rearrange("p (h d) -> p h d", d=128),
                                                          AX.X, ALU.add),
                                  reads=["sqsc"], writes=[f"ss{P}"])
                        body("dv", dvs)

                    def rstd(e, P=P):
                        tr.op(lambda: e.activation(ln8_s[:], ss_s[:, P, :], AF.Ln),
                              reads=[f"ss{P}"], writes=["ln8"])
                        tr.op(lambda: e.activation(sr8_s[:], ln8_s[:], AF.Exp, scale=0.5),
                              reads=["ln8"], writes=["sr8"])
                    body("ac", rstd)

                    def rstd2(e, P=P):
                        tr.op(lambda: e.tensor_scalar(sr8_s[:], sr8_s[:], 1e-8, None, ALU.add),
                              reads=["sr8"], writes=["sr8"])
                        tr.op(lambda: e.reciprocal(rs8_s[:, P, :], sr8_s[:]),
                              reads=["sr8"], writes=[f"rs8{P}"])
                        for fh in range(2):
                            bank = (P * 2 + fh) % 4
                            tr.op(lambda fh=fh, bank=bank: e.tensor_tensor(
                                tm_s[:, P, 4 * fh:4 * fh + 4, :],
                                PS[bank][:].rearrange("p (h d) -> p h d", d=128),
                                rs8_s[:, P, 4 * fh:4 * fh + 4].broadcast_to([128, 4, 128]),
                                ALU.mult),
                                reads=[f"PS{bank}", f"rs8{P}"], writes=[f"tm{P}"], drain=True)
                    body("dv", rstd2)

                def vout(e, tb=tb, tt=tt):
                    for h in range(8):
                        tr.op(lambda h=h: e.dma_start(
                            out=vhat[h, 512 * tb + 128 * tt:512 * tb + 128 * (tt + 1), :],
                            in_=tm_s[:, 2, h, :]),
                              reads=["tm2"], writes=["vhat"], semname="gq", step=16)
                body("gq", vout)

                for P, stg in ((0, stq_s), (1, stk_s)):
                    for h in range(8):
                        psel = 4 + (h % 2)

                        def ptr(e, P=P, h=h, psel=psel):
                            tr.op(lambda: e.transpose(PS[psel][:, 0:128], tm_s[:, P, h, :], ident_s[:]),
                                  reads=[f"tm{P}", "ident"], writes=[f"PS{psel}"])
                        body("pe", ptr)

                        def pcp(e, P=P, h=h, tt=tt, psel=psel, stg=stg):
                            tr.op(lambda: e.tensor_copy(stg[:, h, 128 * tt:128 * (tt + 1)],
                                                        PS[psel][:, 0:128]),
                                  reads=[f"PS{psel}"], writes=[f"st{P}"], drain=True)
                        body("dv", pcp)

            def stout(e, tb=tb):
                for P, (stg, dst) in enumerate(((stq_s, qhatT), (stk_s, khatT))):
                    for h in range(8):
                        tr.op(lambda h=h, stg=stg, dst=dst: e.dma_start(
                            out=dst[h, :, 512 * tb:512 * (tb + 1)], in_=stg[:, h, :]),
                              reads=[f"st{P}"], writes=["qkhatT"], semname="sq", step=16)
            body("sq", stout)

        barrier()
        stA.close()

        stB = contextlib.ExitStack()
        kbuf_s = [mksb(stB, f"kbuf{i}_s", [128, T], BF16) for i in range(2)]
        vbuf_s = [mksb(stB, f"vbuf{i}_s", [128, 32, 129], BF16) for i in range(2)]
        qbuf_s = [mksb(stB, f"qbuf{i}_s", [128, 1024], BF16) for i in range(2)]
        ebuf_s = [mksb(stB, f"ebuf{i}_s", [128, 512], BF16) for i in range(2)]
        rcp_s = mksb(stB, "rcp_s", [128, 1], F32)
        sv_s = mksb(stB, "sv_s", [128, 1], F32)
        attnb_s = mksb(stB, "attnb_s", [128, 8, 1024], F32)

        def vones(e):
            for i in range(2):
                tr.op(lambda i=i: e.memset(vbuf_s[i][:, :, 128:129], 1.0), writes=[f"vones{i}"], drain=True)
        body("dv", vones)

        def rearr(e):
            for r in range(8):
                tr.op(lambda r=r: e.dma_start(out=wfcT[:, 1024 * r:1024 * (r + 1)],
                                              in_=gfc.ap()[2048 * r:2048 * (r + 1), :]),
                      reads=["gfc"], writes=["wfcT"], semname="gq", step=16)
            for r in range(8):
                for q4 in range(4):
                    tr.op(lambda r=r, q4=q4: e.dma_start(
                        out=wprojT[2048 * q4:2048 * (q4 + 1), 256 * r:256 * (r + 1)],
                        in_=gproj.ap()[8192 * r + 2048 * q4:8192 * r + 2048 * (q4 + 1), :]),
                          reads=["gproj"], writes=["wprojT"], semname="gq", step=16)
        body("gq", rearr)

        for sbi in range(4):
            kmax = (sbi + 1) * 1024
            ktall = kmax // 128
            for h in range(8):
                sel = (sbi * 8 + h) % 2

                def lda2(e, h=h, sel=sel, kmax=kmax, ktall=ktall, sbi=sbi):
                    tr.op(lambda: e.dma_start(out=kbuf_s[sel][:, 0:kmax], in_=khatT[h, :, 0:kmax]),
                          reads=["qkhatT"], writes=[f"kbuf{sel}"], semname="sq", step=16)
                    tr.op(lambda: e.dma_start(out=vbuf_s[sel][:, 0:ktall, 0:128],
                                              in_=vhat[h, 0:kmax, :].rearrange("(j p) d -> p j d", p=128)),
                          reads=["vhat"], writes=[f"vbuf{sel}"], semname="sq", step=16)
                    tr.op(lambda: e.dma_start(out=qbuf_s[sel][:],
                                              in_=qhatT[h, :, 1024 * sbi:1024 * (sbi + 1)]),
                          reads=["qkhatT"], writes=[f"qbuf{sel}"], semname="sq", step=16)
                body("sq", lda2)

                for qb in range(2):
                    q0g = 1024 * sbi + 512 * qb
                    kt = (q0g + 512) // 128
                    dstart = q0g // 128
                    for j in range(kt):
                        es = j % 2

                        def smm(e, sel=sel, j=j, es=es, qb=qb):
                            tr.op(lambda: e.matmul(PS[es][:], kbuf_s[sel][:, 128 * j:128 * (j + 1)],
                                                   qbuf_s[sel][:, 512 * qb:512 * (qb + 1)],
                                                   start=True, stop=True),
                                  reads=[f"kbuf{sel}", f"qbuf{sel}"], writes=[f"PS{es}"])
                        body("pe", smm)

                        def eexp(e, es=es):
                            tr.op(lambda: e.activation(ebuf_s[es][:], PS[es][:], AF.Exp,
                                                       scale=tau_s[:, 0:1], bias=ntau_s[:, 0:1]),
                                  reads=[f"PS{es}", "tau", "ntau"], writes=[f"ebuf{es}"], drain=True)
                        body("ac", eexp)

                        if j >= dstart:
                            s_al = j - dstart

                            def emask(e, es=es, s_al=s_al):
                                tr.op(lambda: e.tensor_tensor(
                                    ebuf_s[es][:], ebuf_s[es][:],
                                    mask_s[:, 384 - 128 * s_al:896 - 128 * s_al], ALU.mult),
                                      reads=[f"ebuf{es}", "mask"], writes=[f"ebuf{es}"], drain=True)
                            body("dv", emask)

                        def pvm(e, sel=sel, j=j, es=es, kt=kt):
                            for i in range(4):
                                bank = (2, 3, 6, 7)[i]
                                off = 0
                                tr.op(lambda i=i, bank=bank, off=off: e.matmul(
                                    PS[bank][:, off:off + 129],
                                    ebuf_s[es][:, 128 * i:128 * (i + 1)],
                                    vbuf_s[sel][:, j, :],
                                    start=(j == 0), stop=(j == kt - 1)),
                                      reads=[f"ebuf{es}", f"vbuf{sel}", f"vones{sel}"],
                                      writes=[f"PS{bank}"])
                        body("pe", pvm)

                    for i in range(4):
                        bank = (2, 3, 6, 7)[i]

                        def norm_dv(e, bank=bank):
                            tr.op(lambda: e.reciprocal(rcp_s[:], PS[bank][:, 128:129]),
                                  reads=[f"PS{bank}"], writes=["rcp"])
                            tr.op(lambda: e.tensor_tensor(sv_s[:], rcp_s[:], nu_s[:], ALU.mult),
                                  reads=["rcp", "nu"], writes=["sv"])
                        body("dv", norm_dv)

                        def norm_ac(e, bank=bank, qb=qb, i=i, h=h):
                            tr.op(lambda: e.activation(attnb_s[:, 4 * qb + i, 128 * h:128 * (h + 1)],
                                                       PS[bank][:, 0:128], AF.Copy,
                                                       scale=sv_s[:, 0:1]),
                                  reads=[f"PS{bank}", "sv"], writes=["attnb"])
                        body("ac", norm_ac)

            def attn_out(e, sbi=sbi):
                tr.op(lambda: e.dma_start(out=attnc.ap()[1024 * sbi:1024 * (sbi + 1), :]
                                          .rearrange("(tt p) f -> p tt f", p=128),
                                          in_=attnb_s[:]),
                      reads=["attnb"], writes=["attnc"], semname="gq", step=16)
            body("gq", attn_out)

        def ag_attn(e):
            tr.op(lambda: e.collective_compute("AllGather", ALU.bypass,
                                               replica_groups=[[0, 1], [2, 3], [4, 5], [6, 7]],
                                               ins=[attnc.ap().opt()], outs=[attng.ap().opt()]),
                  reads=["attnc"], writes=["attng"], semname="cc")
        body("gq", ag_attn)

        barrier()
        stB.close()

        stF = contextlib.ExitStack()
        g_s = mksb(stF, "g_s", [128, 64, 512], BF16)
        x1_s = mksb(stF, "x1_s", [128, 4, 2048], F32)
        r2_s = mksb(stF, "r2_s", [128, 4, 2048], F32)
        x1T_s = mksb(stF, "x1T_s", [128, 16, 512], BF16)
        fst_s = [mksb(stF, f"fst{i}_s", [128, 16, 256], BF16) for i in range(2)]
        xr_s = mksb(stF, "xr_s", [128, 2048], F32)
        abA_s = mksb(stF, "abA_s", [128, 2048], BF16)
        abB_s = mksb(stF, "abB_s", [128, 2048], BF16)
        r1_s = mksb(stF, "r1_s", [128, 2048], F32)
        ld_s = mksb(stF, "ld_s", [128, 2048], F32)
        rl_s = mksb(stF, "rl_s", [128, 512], F32)
        pst_s = [mksb(stF, f"pst{i}_s", [128, 1024], BF16) for i in range(2)]
        lsum_s = mksb(stF, "lsum_s", [128, 1], F32)
        lnm_s = mksb(stF, "lnm_s", [128, 1], F32)
        lss_s = mksb(stF, "lss_s", [128, 1], F32)
        lv_s = mksb(stF, "lv_s", [128, 1], F32)
        lr_s = mksb(stF, "lr_s", [128, 1], F32)

        def ln_tail(e, src_key, dst_ap, dst_key):
            tr.op(lambda: e.activation(lr_s[:], lv_s[:], AF.Ln),
                  reads=["lv"], writes=["lr"])
            tr.op(lambda: e.activation(lr_s[:], lr_s[:], AF.Exp, scale=-0.5),
                  reads=["lr"], writes=["lr"])
            tr.op(lambda: e.activation(dst_ap, ld_s[:], AF.Copy, scale=lr_s[:, 0:1]),
                  reads=["ld", "lr"], writes=[dst_key], drain=True)

        for tb4 in range(4):
            tok0 = 512 * tb4

            for tt in range(4):
                r0 = tok0 + 128 * tt

                def ldf(e, r0=r0):
                    tr.op(lambda: e.dma_start(out=xr_s[:], in_=xrows[r0:r0 + 128, :]),
                          writes=["xr"], semname="sq", step=16)
                    for hp in range(2):
                        tr.op(lambda hp=hp: e.dma_start(
                            out=abA_s[:, 1024 * hp:1024 * (hp + 1)],
                            in_=attng.ap()[T * hp + r0:T * hp + r0 + 128, :]),
                              reads=["attng"], writes=["abA"], semname="sq", step=16)
                        tr.op(lambda hp=hp: e.dma_start(
                            out=abB_s[:, 1024 * hp:1024 * (hp + 1)],
                            in_=attng.ap()[T * hp + 2048 + r0:T * hp + 2048 + r0 + 128, :]),
                              reads=["attng"], writes=["abB"], semname="sq", step=16)
                body("sq", ldf)

                def ln1(e, tt=tt):
                    tr.op(lambda: e.scalar_tensor_tensor(r1_s[:], abA_s[:], msel_s[:, 0:1],
                                                         xr_s[:], ALU.mult, ALU.add),
                          reads=["abA", "msel", "xr"], writes=["r1"])
                    tr.op(lambda: e.scalar_tensor_tensor(r1_s[:], abB_s[:], msel_s[:, 1:2],
                                                         r1_s[:], ALU.mult, ALU.add),
                          reads=["abB", "msel", "r1"], writes=["r1"])
                    tr.op(lambda: e.reduce_sum(lsum_s[:], r1_s[:], AX.X),
                          reads=["r1"], writes=["lsum"])
                    tr.op(lambda: e.tensor_scalar(lnm_s[:], lsum_s[:], -1.0 / 2048.0, None, ALU.mult),
                          reads=["lsum"], writes=["lnm"])
                    tr.op(lambda: e.tensor_scalar(ld_s[:], r1_s[:], lnm_s[:, 0:1], None, ALU.add),
                          reads=["r1", "lnm"], writes=["ld"])
                body("dv", ln1)

                def ln1b(e):
                    tr.op(lambda: e.activation(r1_s[:], ld_s[:], AF.Square),
                          reads=["ld"], writes=["r1"])
                body("ac", ln1b)

                def ln1c(e):
                    tr.op(lambda: e.reduce_sum(lss_s[:], r1_s[:], AX.X),
                          reads=["r1"], writes=["lss"])
                    tr.op(lambda: e.tensor_scalar(lv_s[:], lss_s[:], 1.0 / 2048.0, 1e-5,
                                                  ALU.mult, ALU.add),
                          reads=["lss"], writes=["lv"])
                body("dv", ln1c)

                def ln1d(e, tt=tt):
                    ln_tail(e, "ld", x1_s[:, tt, :], "x1")
                body("ac", ln1d)

                for i in range(16):
                    psel = 4 + (i % 2)

                    def xtr(e, i=i, tt=tt, psel=psel):
                        tr.op(lambda: e.transpose(PS[psel][:, 0:128], x1_s[:, tt, 128 * i:128 * (i + 1)],
                                                  ident_s[:]),
                              reads=["x1", "ident"], writes=[f"PS{psel}"])
                    body("pe", xtr)

                    def xcp(e, i=i, tt=tt, psel=psel):
                        tr.op(lambda: e.tensor_copy(x1T_s[:, i, 128 * tt:128 * (tt + 1)],
                                                    PS[psel][:, 0:128]),
                              reads=[f"PS{psel}"], writes=["x1T"], drain=True)
                    body("dv", xcp)

            for ob in range(32):
                fs2 = ob % 2

                def ldw1(e, ob=ob, fs2=fs2):
                    tr.op(lambda: e.dma_start(out=fst_s[fs2][:],
                                              in_=wfcT[:, 256 * ob:256 * (ob + 1)]
                                              .rearrange("(i p) o -> p i o", p=128)),
                          reads=["wfcT"], writes=[f"fst{fs2}"], semname="sq", step=16)
                body("sq", ldw1)

                for oc in range(2):
                    hs = oc % 2

                    def fmm(e, fs2=fs2, oc=oc, hs=hs):
                        for i in range(16):
                            tr.op(lambda i=i: e.matmul(PS[hs][:],
                                                       fst_s[fs2][:, i, 128 * oc:128 * (oc + 1)],
                                                       x1T_s[:, i, :],
                                                       start=(i == 0), stop=(i == 15)),
                                  reads=[f"fst{fs2}", "x1T"], writes=[f"PS{hs}"])
                    body("pe", fmm)

                    def frelu(e, hs=hs, ob=ob, oc=oc):
                        tr.op(lambda: e.activation(rl_s[:], PS[hs][:], AF.Relu),
                              reads=[f"PS{hs}"], writes=["rl"])
                        tr.op(lambda: e.activation(g_s[:, 2 * ob + oc, :], rl_s[:], AF.Square),
                              reads=["rl"], writes=["g"], drain=True)
                    body("ac", frelu)

            for cp in range(2):
                for oc in range(64):
                    ps2 = oc % 2

                    def ldw2(e, cp=cp, oc=oc, ps2=ps2):
                        tr.op(lambda: e.dma_start(out=pst_s[ps2][:],
                                                  in_=wprojT[128 * oc:128 * (oc + 1),
                                                             1024 * cp:1024 * (cp + 1)]),
                              reads=["wprojT"], writes=[f"pst2{ps2}"], semname="sq", step=16)
                    body("sq", ldw2)

                    def ymm(e, oc=oc, ps2=ps2):
                        for tt in range(4):
                            for hf in range(2):
                                bank = tt * 2 + hf
                                tr.op(lambda tt=tt, hf=hf, bank=bank: e.matmul(
                                    PS[bank][:],
                                    g_s[:, oc, 128 * tt:128 * (tt + 1)],
                                    pst_s[ps2][:, 512 * hf:512 * (hf + 1)],
                                    start=(oc == 0), stop=(oc == 63)),
                                      reads=["g", f"pst2{ps2}"], writes=[f"PS{bank}"])
                    body("pe", ymm)

                def yevac(e, cp=cp):
                    for tt in range(4):
                        for hf in range(2):
                            bank = tt * 2 + hf
                            col = 1024 * cp + 512 * hf
                            tr.op(lambda tt=tt, bank=bank, col=col: e.tensor_tensor(
                                r2_s[:, tt, col:col + 512], PS[bank][:],
                                x1_s[:, tt, col:col + 512], ALU.add),
                                  reads=[f"PS{bank}", "x1"], writes=["r2"])
                body("dv", yevac)

            for tt in range(4):
                def ln2(e, tt=tt):
                    tr.op(lambda: e.reduce_sum(lsum_s[:], r2_s[:, tt, :], AX.X),
                          reads=["r2"], writes=["lsum"])
                    tr.op(lambda: e.tensor_scalar(lnm_s[:], lsum_s[:], -1.0 / 2048.0, None, ALU.mult),
                          reads=["lsum"], writes=["lnm"])
                    tr.op(lambda: e.tensor_scalar(ld_s[:], r2_s[:, tt, :], lnm_s[:, 0:1], None,
                                                  ALU.add),
                          reads=["r2", "lnm"], writes=["ld"])
                body("dv", ln2)

                def ln2b(e):
                    tr.op(lambda: e.activation(r1_s[:], ld_s[:], AF.Square),
                          reads=["ld"], writes=["r1"])
                body("ac", ln2b)

                def ln2c(e):
                    tr.op(lambda: e.reduce_sum(lss_s[:], r1_s[:], AX.X),
                          reads=["r1"], writes=["lss"])
                    tr.op(lambda: e.tensor_scalar(lv_s[:], lss_s[:], 1.0 / 2048.0, 1e-5,
                                                  ALU.mult, ALU.add),
                          reads=["lss"], writes=["lv"])
                body("dv", ln2c)

                def ln2d(e):
                    ln_tail(e, "ld", r1_s[:], "r1")
                body("ac", ln2d)

                def stf(e, tt=tt, tok0=tok0):
                    tr.op(lambda: e.dma_start(out=out[tok0 + 128 * tt:tok0 + 128 * (tt + 1), :],
                                              in_=r1_s[:]),
                          reads=["r1"], writes=["out"], semname="so", step=16)
                body("sq", stf)

        barrier()
        stF.close()

        def fini(e):
            tr.wait("so", tr.sems["so"].n)
        body("sq", fini)

    if tr.dry:
        tr.next_pass()
        print(f"[kernel] pass1: {len(tr.need_flush)} cross-engine flush sites")
        return build_nc(tr)
    ninst = sum(1 for _ in nc.all_instructions())
    print(f"[kernel] built: {ninst} instructions")
    return nc


_NC_CACHE = None


def kernel(L0_attn_x, L0_attn_wq, L0_attn_wk, L0_attn_wv, L0_attn_tau, L0_attn_nu,
           L0_ffn_w_fc, L0_ffn_w_proj):
    global _NC_CACHE
    x = np.asarray(L0_attn_x, dtype=np.float32)
    wq = np.asarray(L0_attn_wq, dtype=np.float32)
    wk = np.asarray(L0_attn_wk, dtype=np.float32)
    wv = np.asarray(L0_attn_wv, dtype=np.float32)
    tau = np.asarray(L0_attn_tau, dtype=np.float32).reshape(-1)[0]
    nu = np.asarray(L0_attn_nu, dtype=np.float32).reshape(-1)[0]
    wfc = np.asarray(L0_ffn_w_fc, dtype=np.float32)
    wproj = np.asarray(L0_ffn_w_proj, dtype=np.float32)

    xT_b = [np.ascontiguousarray(x[b].T) for b in range(B)]
    wT = [np.ascontiguousarray(wq.T), np.ascontiguousarray(wk.T), np.ascontiguousarray(wv.T)]
    taur = np.ascontiguousarray(np.broadcast_to(np.float32(tau), (128, 1)))
    nur = np.ascontiguousarray(np.broadcast_to(np.float32(nu), (128, 1)))
    mask = (np.arange(896)[None, :] - 384 >= np.arange(128)[:, None]).astype(ml_dtypes.bfloat16)
    ident = np.eye(128, dtype=np.float32)
    shrk = np.ascontiguousarray(np.broadcast_to(
        (np.float32(1.0) / np.asarray(SHRINKS, np.float32))[None, :], (128, 16)))
    shrs = np.ascontiguousarray(np.broadcast_to(np.asarray(SHRINKS, np.float32)[None, :], (128, 16)))

    in_maps = []
    for c in range(NC):
        b, hh = c // 2, c % 2
        ms = np.zeros((128, 2), np.float32)
        ms[:, 0] = 1.0 - hh
        ms[:, 1] = float(hh)
        in_maps.append(dict(
            xT=xT_b[b],
            xrows=np.ascontiguousarray(x[b, TOK * hh:TOK * (hh + 1)]),
            wqT=np.ascontiguousarray(wT[0][:, FS * hh:FS * (hh + 1)]),
            wkT=np.ascontiguousarray(wT[1][:, FS * hh:FS * (hh + 1)]),
            wvT=np.ascontiguousarray(wT[2][:, FS * hh:FS * (hh + 1)]),
            wfcq=np.ascontiguousarray(wfc[1024 * c:1024 * (c + 1)]),
            wprojq=np.ascontiguousarray(wproj[256 * c:256 * (c + 1)]),
            taur=taur, nur=nur, msel=ms, maskin=mask, identin=ident, shrk=shrk, shrs=shrs,
        ))

    if _NC_CACHE is None:
        _NC_CACHE = build_nc()
    res = bass_utils.run_bass_kernel_spmd(_NC_CACHE, in_maps, core_ids=list(range(NC)))

    y = np.empty((B, T, C), dtype=np.float32)
    for c in range(NC):
        b, hh = c // 2, c % 2
        y[b, TOK * hh:TOK * (hh + 1)] = res.results[c]["out"]
    return y
